# version 35
# baseline (speedup 1.0000x reference)
import numpy as np
from contextlib import ExitStack

import concourse.bass as bass
import concourse.mybir as mybir
from concourse.bass_utils import run_bass_kernel_spmd

F32 = mybir.dt.float32
BF16 = mybir.dt.bfloat16
F16 = mybir.dt.float16
AF = mybir.ActivationFunctionType
AX = mybir.AxisListType
ALU = mybir.AluOpType

S = 2048
D = 2048
H = 16
E = 128
FH = 5461
FP = 5504
NFC = 43
NSC = 16
QSCALE = 1.0 / float(np.sqrt(128.0))
SMAX = 24.0
EPS = 1e-5

ENGINES = ("tensor", "vector", "scalar", "gpsimd", "sync")


class Sched:
    def __init__(self):
        self.ops = []
        self.counts = {}
        self.seen = {e: {} for e in ENGINES}
        self.esem = {"tensor": "e_t", "vector": "e_v", "scalar": "e_s",
                     "gpsimd": "e_g", "sync": "e_y"}
        self.dsem = {"sync": "d_y", "scalar": "d_s", "gpsimd": "d_g"}

    def add(self, engine, fn, after=(), sig=False, dma=False, fence=False):
        waits = []
        for ev in after:
            if ev is None:
                continue
            sem, val = ev
            if sem == self.esem[engine] and not fence:
                continue
            prev = self.seen[engine].get(sem, -1)
            if val > prev:
                waits.append((sem, val))
                self.seen[engine][sem] = val
        ev_out = None
        if dma:
            sem = self.dsem[engine]
            self.counts[sem] = self.counts.get(sem, 0) + 16
            ev_out = (sem, self.counts[sem])
        elif sig:
            sem = self.esem[engine]
            self.counts[sem] = self.counts.get(sem, 0) + 1
            ev_out = (sem, self.counts[sem])
        self.ops.append((engine, fn, waits, ev_out, dma))
        return ev_out

    def check(self):
        per_eng = {e: [op for op in self.ops if op[0] == e] for e in ENGINES}
        idx = {e: 0 for e in ENGINES}
        counts = {}
        total = len(self.ops)
        done = 0
        while done < total:
            progress = False
            for e in ENGINES:
                while idx[e] < len(per_eng[e]):
                    _, _, waits, inc, _ = per_eng[e][idx[e]]
                    if all(counts.get(s_, 0) >= v_ for s_, v_ in waits):
                        if inc is not None:
                            s_, v_ = inc
                            counts[s_] = max(counts.get(s_, 0), v_)
                        idx[e] += 1
                        done += 1
                        progress = True
                    else:
                        break
            if not progress:
                stuck = {e: (per_eng[e][idx[e]][2] if idx[e] < len(per_eng[e]) else "done")
                         for e in ENGINES}
                raise RuntimeError(f"deadlock: {stuck}")


def build_nc(debug=False):
    nc = bass.Bass("TRN2", target_bir_lowering=False)
    KIND = "ExternalOutput" if debug else "Internal"

    x_in = nc.dram_tensor("x", [S, D], F32, kind="ExternalInput").ap()
    wq_in = nc.dram_tensor("wq", [H, E, D], F32, kind="ExternalInput").ap()
    wk_in = nc.dram_tensor("wk", [H, E, D], F32, kind="ExternalInput").ap()
    wv_in = nc.dram_tensor("wv", [H, E, D], F32, kind="ExternalInput").ap()
    wo_in = nc.dram_tensor("wo", [H, D, E], F32, kind="ExternalInput").ap()
    w1_in = nc.dram_tensor("w1", [FH, D], F32, kind="ExternalInput").ap()
    w3_in = nc.dram_tensor("w3", [FH, D], F32, kind="ExternalInput").ap()
    w2_in = nc.dram_tensor("w2", [D, FH], F32, kind="ExternalInput").ap()
    out_hbm = nc.dram_tensor("out", [S, D], F32, kind="ExternalOutput").ap()

    w1qT_s = nc.dram_tensor("w1qT_s", [NFC, 128, D], BF16, kind=KIND).ap()
    w3qT_s = nc.dram_tensor("w3qT_s", [NFC, 128, D], BF16, kind=KIND).ap()
    w2qT_s = nc.dram_tensor("w2qT_s", [NFC, 128, D], BF16, kind=KIND).ap()
    y32_s = nc.dram_tensor("y32_s", [S, D], F32, kind=KIND).ap()

    es = ExitStack()
    sch = Sched()

    with es:
        A1 = es.enter_context(nc.sbuf_tensor("A1", [128, 32768], BF16))
        A2 = es.enter_context(nc.sbuf_tensor("A2", [128, 32768], BF16))
        A3 = es.enter_context(nc.sbuf_tensor("A3", [128, 28672], BF16))
        smalls = es.enter_context(nc.sbuf_tensor("smalls", [128, 640], F32))
        smallsb = es.enter_context(nc.sbuf_tensor("smallsb", [128, 256], BF16))

        def a3(c0, n):
            return A3[:, c0:c0 + n]

        def a3f(c0, nf):
            return A3[:, c0:c0 + 2 * nf].bitcast(F32)

        def a2(c0, n):
            return A2[:, c0:c0 + n]

        def a2f(c0, nf):
            return A2[:, c0:c0 + 2 * nf].bitcast(F32)

        ones128 = smalls[:, 0:128]
        maskdiag = smalls[:, 128:256]
        stat = smalls[:, 256:512]
        iota_p = smalls[:, 512:640]
        identb = smallsb[:, 0:128]
        mask_bf = smallsb[:, 128:256]

        ST_NEGSMAX = 0
        ST_THR = {1: 1, 3: 2, 2: 3}
        ST_NEGTHR = {1: 4, 3: 5, 2: 6}
        ST_S23 = 7
        ST_SM = 8
        ST_LN = 24
        ST_ABS = 44
        ST_TOT = 85
        ST_SF = {1: 89, 3: 90, 2: 91}

        def stc(i, n=1):
            return stat[:, i:i + n]

        ps_sc = es.enter_context(nc.psum_tensor("ps_sc", [128, 2048], F32))
        ps_pt = es.enter_context(nc.psum_tensor("ps_pt", [128, 2048], BF16))
        ps_z = es.enter_context(nc.psum_tensor("ps_z", [128, 512], F32))
        ps_ms = es.enter_context(nc.psum_tensor("ps_ms", [128, 512], F32))
        ps_h3 = ps_ms[:, 0:512]

        sems = {}
        for s_name in ("e_t", "e_v", "e_s", "e_g", "e_y", "d_y", "d_s", "d_g"):
            sems[s_name] = es.enter_context(nc.semaphore(s_name))

        ev_if = sch.add("gpsimd", lambda g: g.iota(maskdiag, [[1, 128]], channel_multiplier=0,
                                                   allow_small_or_imprecise_dtypes=True), sig=True)
        ev_ip = sch.add("gpsimd", lambda g: g.iota(iota_p, [[0, 128]], channel_multiplier=1,
                                                   allow_small_or_imprecise_dtypes=True), sig=True)
        ev_id = sch.add("vector", lambda v: v.tensor_tensor(identb, maskdiag, iota_p, ALU.is_equal),
                        after=[ev_if, ev_ip], sig=True)
        ev_mk0 = sch.add("vector", lambda v: v.tensor_tensor(maskdiag, maskdiag, iota_p, ALU.is_gt),
                         after=[ev_id], sig=True, fence=True)
        ev_mask = sch.add("vector", lambda v: v.tensor_scalar_mul(maskdiag, maskdiag, -1.0e9),
                          after=[ev_mk0], sig=True, fence=True)
        ev_maskb = sch.add("vector", lambda v: v.tensor_copy(mask_bf, maskdiag),
                           after=[ev_mask], sig=True, fence=True)
        sch.add("vector", lambda v: v.memset(ones128, 1.0))
        sch.add("vector", lambda v: v.memset(stc(ST_TOT, 3), 0.0))
        ev_cst = sch.add("vector", lambda v: v.memset(stc(ST_NEGSMAX), -float(SMAX)), sig=True)

        RAW0 = 22528
        x_done = []
        prev_xtr = [None, None]
        for sc in range(NSC):
            par = sc % 2
            buf = a3(RAW0 + par * 2048, 2048)
            ev_ld = sch.add("gpsimd",
                            (lambda b, s0: lambda g: g.dma_start(b, x_in[s0:s0 + 128, :]))(buf, sc * 128),
                            after=[prev_xtr[par]], dma=True)
            out_ap = bass.AP(A2, sc * 128, [[32768, 128], [2048, 16], [1, 128]])
            ev_tr = sch.add("scalar",
                            (lambda o, b: lambda s: s.dma_start(o, b, transpose=True))(out_ap, buf),
                            after=[ev_ld], dma=True)
            prev_xtr[par] = ev_tr
            x_done.append(ev_tr)
        x_ready = x_done[-1]

        QIN0, QOUT0, TB0 = 16384, 24576, 26624
        PBF1 = 20480
        prev_zev_slot = [[], []]
        pbe = [prev_xtr[0], prev_xtr[1]]
        pbe = [None, None]
        prep = {"t": {}, "n": {}, "q": {}, "s23": None}

        def prepass_jobs():
            prev_ac = None
            prev_thr = None
            k_abs = [0]

            def abs_tiles(w_ap, nrows, ncols, tag):
                nonlocal prev_ac
                for rt in range((nrows + 127) // 128):
                    r0 = rt * 128
                    rn = min(128, nrows - r0)
                    for ct in range((ncols + 1023) // 1024):
                        c0 = ct * 1024
                        cn = min(1024, ncols - c0)
                        par = k_abs[0] % 2
                        buf = a3f(QIN0 + par * 2048, 1024)
                        ev_ld = sch.add("sync",
                                        (lambda b, r0_, rn_, c0_, cn_: lambda y: y.dma_start(
                                            b[0:rn_, 0:cn_], w_ap[r0_:r0_ + rn_, c0_:c0_ + cn_]))(
                                            buf, r0, rn, c0, cn),
                                        after=[pbe[par]], dma=True)
                        col = ST_ABS + (k_abs[0] % 40)
                        ev_rd = sch.add("vector",
                                        (lambda b, rn_, cn_, cl: lambda v: v.reduce_sum(
                                            stc(cl)[0:rn_, :], b[0:rn_, 0:cn_], axis=AX.X,
                                            apply_absolute_value=True))(buf, rn, cn, col),
                                        after=[ev_ld], sig=True)
                        ev_ac = sch.add("vector",
                                        (lambda cl, rn_, tg: lambda v: v.tensor_add(
                                            stc(ST_TOT + tg)[0:rn_, :], stc(ST_TOT + tg)[0:rn_, :],
                                            stc(cl)[0:rn_, :]))(col, rn, tag),
                                        after=[ev_rd, prev_ac], sig=True, fence=True)
                        pbe[par] = ev_rd
                        prev_ac = ev_ac
                        k_abs[0] += 1
                        yield
                return

            def thr_chain(tag, m, total):
                nonlocal prev_thr, prev_ac
                ev_mm = sch.add("tensor",
                                (lambda tg: lambda t: t.matmul(ps_ms[:, 0:1], ones128,
                                                               stc(ST_TOT + tg), start=True,
                                                               stop=True))(tag),
                                after=[prev_ac, ev_cst, prev_thr] + prev_zev_slot[1], sig=True)
                ev_t = sch.add("scalar",
                               (lambda m_, sc_: lambda s: s.activation(stc(ST_THR[m_]),
                                                                       ps_ms[:, 0:1],
                                                                       AF.Identity, scale=sc_))(
                                   m, 0.5 / float(total)),
                               after=[ev_mm], sig=True)
                ev_n = sch.add("vector",
                               (lambda m_: lambda v: v.tensor_scalar_mul(stc(ST_NEGTHR[m_]),
                                                                         stc(ST_THR[m_]), -1.0))(m),
                               after=[ev_t], sig=True)
                ev_sf = sch.add("vector",
                                (lambda m_: lambda v: v.tensor_scalar_mul(stc(ST_SF[m_]),
                                                                          stc(ST_THR[m_]), 2.0))(m),
                                after=[ev_t], sig=True)
                prep["t"][m] = ev_t
                prep["n"][m] = ev_n
                prep["sf"] = prep.get("sf", {})
                prep["sf"][m] = ev_sf
                prev_zev_slot[1].append(ev_t)
                prev_thr = ev_t

            yield from abs_tiles(w1_in, FH, D, 0)
            thr_chain(0, 1, FH * D)
            yield
            yield from abs_tiles(w3_in, FH, D, 1)
            thr_chain(1, 3, FH * D)
            yield
            yield from abs_tiles(w2_in, D, FH, 2)
            thr_chain(2, 2, D * FH)
            prep["s23"] = sch.add("vector",
                                  lambda v: v.tensor_mul(stc(ST_S23), stc(ST_THR[2]),
                                                         stc(ST_THR[3])),
                                  after=[prep["t"][2], prep["t"][3]], sig=True, fence=True)
            yield

            prev_out = [None, None]
            prev_tb = [None, None]
            kq = [0]
            prep["grp"] = {}
            prep["qt"] = {}

            def quant_tile(w_ap, dst, nrows, ncols, m, kind, rt, ct, grp_evs):
                r0 = rt * 128
                rn = max(0, min(128, nrows - r0))
                rpad = 128
                c0 = ct * 1024
                cn = max(0, min(1024, ncols - c0))
                cpad = min(1024, (D if kind == "fd" else FP) - c0)
                thr_ev, nthr_ev = prep["t"][m], prep["n"][m]
                par = kq[0] % 2
                buf = a3f(QIN0 + par * 2048, 1024)
                obuf = a3(QOUT0 + par * 1024, 1024)
                tbuf = a3(TB0 + par * 1024, 1024)
                if rn > 0 and cn > 0:
                    ev_ld = sch.add("sync",
                                    (lambda b, r0_, rn_, c0_, cn_: lambda y: y.dma_start(
                                        b[0:rn_, 0:cn_], w_ap[r0_:r0_ + rn_, c0_:c0_ + cn_]))(
                                        buf, r0, rn, c0, cn),
                                    after=[pbe[par], thr_ev, nthr_ev, prev_out[par]], dma=True)
                    ev_pre = None
                    if rpad > rn or cpad > cn:
                        ev_pre = sch.add("vector",
                                         (lambda o, rp_, cp_: lambda v: v.memset(
                                             o[0:rp_, 0:cp_], 0.0))(obuf, rpad, cpad),
                                         after=[prev_out[par]], sig=True)
                    if kq[0] % 2 == 1:
                        ev_s1 = sch.add("vector",
                                        (lambda o, b, rn_, cn_, m_: lambda v: v.tensor_scalar(
                                            o[0:rn_, 0:cn_], b[0:rn_, 0:cn_],
                                            stc(ST_THR[m_])[0:rn_, :], 2.0, ALU.is_ge,
                                            ALU.mult))(obuf, buf, rn, cn, m),
                                        after=[ev_ld, ev_pre], sig=True, fence=True)
                        ev_s2 = sch.add("vector",
                                        (lambda b, rn_, cn_, m_: lambda v: v.tensor_scalar(
                                            b[0:rn_, 0:cn_], b[0:rn_, 0:cn_],
                                            stc(ST_NEGTHR[m_])[0:rn_, :], 2.0, ALU.is_le,
                                            ALU.mult))(buf, rn, cn, m),
                                        after=[ev_s1], sig=True, fence=True)
                        ev_q = sch.add("vector",
                                       (lambda o, b, rn_, cn_: lambda v: v.tensor_sub(
                                           o[0:rn_, 0:cn_], o[0:rn_, 0:cn_], b[0:rn_, 0:cn_]))(
                                           obuf, buf, rn, cn),
                                       after=[ev_s2], sig=True, fence=True)
                    else:
                        ev_s1 = sch.add("scalar",
                                        (lambda o, b, rn_, cn_, m_: lambda s: s.activation(
                                            o[0:rn_, 0:cn_], b[0:rn_, 0:cn_], AF.Sign,
                                            bias=stc(ST_NEGTHR[m_])[0:rn_, :]))(obuf, buf, rn, cn, m),
                                        after=[ev_ld, ev_pre], sig=True)
                        ev_s2 = sch.add("scalar",
                                        (lambda b, rn_, cn_, m_: lambda s: s.activation(
                                            b[0:rn_, 0:cn_], b[0:rn_, 0:cn_], AF.Sign,
                                            bias=stc(ST_THR[m_])[0:rn_, :]))(buf, rn, cn, m),
                                        after=[ev_s1], sig=True, fence=True)
                        ev_q = sch.add("vector",
                                       (lambda o, b, rn_, cn_: lambda v: v.tensor_add(
                                           o[0:rn_, 0:cn_], o[0:rn_, 0:cn_], b[0:rn_, 0:cn_]))(
                                           obuf, buf, rn, cn),
                                       after=[ev_s2], sig=True)
                    pbe[par] = ev_q if kq[0] % 2 == 1 else ev_s2
                else:
                    ev_q = sch.add("vector",
                                   (lambda o, rp_, cp_: lambda v: v.memset(
                                       o[0:rp_, 0:cp_], 0.0))(obuf, rpad, cpad),
                                   after=[prev_out[par]], sig=True)
                ncb = cpad // 128
                ev_tr = sch.add("scalar",
                                (lambda t_, o, cp_, nc_: lambda s: s.dma_start(
                                    t_[:, 0:cp_].rearrange("p (c f) -> p c f", c=nc_),
                                    o[0:128, 0:cp_], transpose=True))(tbuf, obuf, cpad, ncb),
                                after=[ev_q, prev_tb[par]], dma=True)
                prev_out[par] = ev_tr
                prep["qt"][m] = ev_tr
                if kind == "fd":
                    ev_sp = sch.add("gpsimd",
                                    (lambda d, t_, rt_, c0_, cp_: lambda g: g.dma_start(
                                        d[rt_][:, c0_:c0_ + cp_], t_[:, 0:cp_]))(
                                        dst, tbuf, rt, c0, cpad),
                                    after=[ev_tr], dma=True)
                else:
                    dst_ap = bass.AP(dst.tensor, dst.offset + (ct * 8) * 128 * D + rt * 128,
                                     [[D, 128], [128 * D, ncb], [1, 128]])
                    ev_sp = sch.add("gpsimd",
                                    (lambda da, t_, cp_, nc_: lambda g: g.dma_start(
                                        da, t_[:, 0:cp_].rearrange("p (c f) -> p c f", c=nc_)))(
                                        dst_ap, tbuf, cpad, ncb),
                                    after=[ev_tr], dma=True)
                prev_tb[par] = ev_sp
                grp_evs.append(ev_sp)
                kq[0] += 1

            for gg in range((NFC + 7) // 8):
                grp_evs = []
                fcs = list(range(gg * 8, min(gg * 8 + 8, NFC)))
                for fc in fcs:
                    for ct in range(2):
                        quant_tile(w1_in, w1qT_s, FH, D, 1, "fd", fc, ct, grp_evs)
                        yield
                for fc in fcs:
                    for ct in range(2):
                        quant_tile(w3_in, w3qT_s, FH, D, 3, "fd", fc, ct, grp_evs)
                        yield
                for rt in range(16):
                    quant_tile(w2_in, w2qT_s, D, FH, 2, "df", rt, gg, grp_evs)
                    yield
                prep["grp"][gg * 2] = grp_evs
                prep["grp"][gg * 2 + 1] = grp_evs


        prep_gen = prepass_jobs()
        prep_left = [True]

        def drive_prepass(n):
            for _ in range(n):
                try:
                    next(prep_gen)
                except StopIteration:
                    prep_left[0] = False
                    return

        WT0 = 0
        K_sb = a3(6144, 2048)
        Q_sb = a3(8192, 2048)
        V_tm = a3(10240, 2048)
        P_bf = a3(12288, 2048)
        V_fm = P_bf
        PT_sb = a3(14336, 2048)

        z_done = {sc: [] for sc in range(NSC)}
        kqv_evts = [None, None]
        prev_p1_half = [None, None]
        prev_pt_parity = [None, None]
        prev_pte_parity = [None, None]
        prev_zmm_parity = [None, None]
        prev_zev_any = [None]
        jcnt = [0]
        prev_kev = None
        prev_qev = None
        prev_vev = None
        prev_vtr = None
        raw_free = [prev_xtr[0], prev_xtr[1]]
        wt_tr = {}

        kqv_free = {"k": None, "q": None, "v": None}

        def emit_w_load(h, after_extra=()):
            for i, (m, src) in enumerate((("k", wk_in[h]), ("q", wq_in[h]), ("v", wv_in[h]))):
                for hf in range(2):
                    par = (2 * (3 * h + i) + hf) % 2
                    raw = a3(RAW0 + par * 1024, 1024)
                    ev_c = sch.add("gpsimd",
                                   (lambda r, s_, hf_: lambda g: g.dma_start(
                                       r, s_[:, hf_ * 1024:(hf_ + 1) * 1024]))(raw, src, hf),
                                   after=[raw_free[par]], dma=True)
                    wt = a3(WT0 + i * 2048 + hf * 1024, 1024)
                    ev_t = sch.add("scalar",
                                   (lambda w, r: lambda s: s.dma_start(
                                       w.rearrange("p (c f) -> p c f", c=8), r, transpose=True))(
                                       wt, raw),
                                   after=[ev_c, kqv_free[m]] + list(after_extra), dma=True)
                    raw_free[par] = ev_t
                    wt_tr[(m, h)] = ev_t

        emit_w_load(0, after_extra=[x_ready])

        for h in range(H):
            wtk = a3(WT0 + 0 * 2048, 2048)
            wtq = a3(WT0 + 1 * 2048, 2048)
            wtv = a3(WT0 + 2 * 2048, 2048)

            def kqv_mm(wt):
                def fn(t):
                    inst = None
                    for dc in range(16):
                        for q4 in range(4):
                            inst = t.matmul(ps_sc[:, q4 * 512:(q4 + 1) * 512],
                                            wt[:, dc * 128:(dc + 1) * 128],
                                            A2[:, dc * 2048 + q4 * 512: dc * 2048 + (q4 + 1) * 512],
                                            start=(dc == 0), stop=(dc == 15))
                    return inst
                return fn

            ev_kmm = sch.add("tensor", kqv_mm(wtk),
                             after=[wt_tr[("k", h)], x_ready, prev_p1_half[0],
                                    prev_p1_half[1], prev_kev], sig=True)
            kqv_free["k"] = ev_kmm
            ev_kev = sch.add("vector", lambda v: v.tensor_copy(K_sb, ps_sc[:, :]),
                             after=[ev_kmm], sig=True)
            prev_kev = ev_kev
            ev_qmm = sch.add("tensor", kqv_mm(wtq), after=[wt_tr[("q", h)], ev_kev, prev_qev],
                             sig=True)
            kqv_free["q"] = ev_qmm
            ev_qev = sch.add("scalar", lambda s: s.activation(Q_sb, ps_sc[:, :], AF.Identity,
                                                              scale=QSCALE),
                             after=[ev_qmm], sig=True)
            prev_qev = ev_qev
            ev_vmm = sch.add("tensor", kqv_mm(wtv), after=[wt_tr[("v", h)], ev_qev, prev_vev],
                             sig=True)
            kqv_free["v"] = ev_vmm
            kqv_evts[h % 2] = ev_vmm
            ev_vev = sch.add("vector", lambda v: v.tensor_copy(V_fm, ps_sc[:, :]),
                             after=[ev_vmm, prev_pt_parity[0]], sig=True)
            prev_vev = ev_vev
            ev_vtr = sch.add("scalar",
                             lambda s: s.dma_start(V_tm.rearrange("p (c f) -> p c f", c=16),
                                                   V_fm, transpose=True),
                             after=[ev_vev, prev_vtr, prev_zmm_parity[0],
                                    prev_zmm_parity[1]], dma=True)
            prev_vtr = ev_vtr

            if h + 1 < H:
                emit_w_load(h + 1)

            P_bufs = [P_bf, a3(PBF1, 2048)]
            PT_bufs = [PT_sb, PT_sb]

            scores_evs = {}
            p2_evs = {}

            def emit_scores(sq):
                kmax = 128 * (sq + 1)
                par = sq % 2
                jobs = [(0, min(1024, kmax))]
                if kmax > 1024:
                    jobs.append((1024, kmax))
                evs = []
                for (c0, c1) in jobs:
                    half = jcnt[0] % 2
                    jcnt[0] += 1
                    psJ = ps_sc[:, half * 1024: half * 1024 + 1024]

                    def fn(sq_, c0_, c1_, half_):
                        def run(t):
                            inst = None
                            psJ_ = ps_sc[:, half_ * 1024: half_ * 1024 + 1024]
                            for cc in range(c0_, c1_, 512):
                                cn = min(512, c1_ - cc)
                                inst = t.matmul(psJ_[:, cc - c0_: cc - c0_ + cn],
                                                Q_sb[:, sq_ * 128:(sq_ + 1) * 128],
                                                K_sb[:, cc:cc + cn], start=True, stop=True)
                            d0 = sq_ * 128
                            if c0_ <= d0 < c1_:
                                inst = t.matmul(psJ_[:, d0 - c0_: d0 - c0_ + 128],
                                                identb, mask_bf, start=False, stop=True)
                            return inst
                        return run
                    ev = sch.add("tensor", fn(sq, c0, c1, half),
                                 after=[prev_p1_half[half], ev_vev, ev_qev, ev_maskb],
                                 sig=True)
                    evs.append((c0, c1, half, ev))
                scores_evs[sq] = evs

            def emit_softmax(sq):
                kmax = 128 * (sq + 1)
                par = sq % 2
                base = ST_SM + 8 * par
                p1s = []
                for j, (c0, c1, half, ev_sc_j) in enumerate(scores_evs[sq]):
                    ev_p1 = sch.add("scalar",
                                    (lambda c0_, c1_, half_, j_, par_: lambda s: s.activation(
                                        P_bufs[par_][:, c0_:c1_],
                                        ps_sc[:, half_ * 1024: half_ * 1024 + (c1_ - c0_)],
                                        AF.Exp, bias=stc(ST_NEGSMAX),
                                        accum_out=stc(base + j_)))(c0, c1, half, j, par),
                                    after=[ev_sc_j, prev_pt_parity[par], ev_vtr, ev_cst],
                                    sig=True)
                    prev_p1_half[half] = ev_p1
                    p1s.append(ev_p1)
                if len(p1s) == 2:
                    ev_rs = sch.add("vector",
                                    (lambda b_: lambda v: v.tensor_add(stc(b_ + 2), stc(b_),
                                                                       stc(b_ + 1)))(base),
                                    after=p1s, sig=True)
                    lnsrc = base + 2
                else:
                    ev_rs = None
                    lnsrc = base
                ev_ln = sch.add("scalar",
                                (lambda b_, l_: lambda s: s.activation(stc(b_ + 3), stc(l_),
                                                                       AF.Ln))(base, lnsrc),
                                after=[ev_rs] + p1s, sig=True)
                ev_rcp = sch.add("scalar",
                                 (lambda b_: lambda s: s.activation(stc(b_ + 4), stc(b_ + 3),
                                                                    AF.Exp, scale=-1.0))(base),
                                 sig=True)
                ev_p2 = sch.add("vector",
                                (lambda kmax_, par_, b_: lambda v: v.tensor_scalar_mul(
                                    P_bufs[par_][:, 0:kmax_], P_bufs[par_][:, 0:kmax_],
                                    stc(b_ + 4)))(kmax, par, base),
                                after=[ev_rcp] + p1s, sig=True)
                p2_evs[sq] = ev_p2

            def emit_tail(sq, h_):
                kmax = 128 * (sq + 1)
                par = sq % 2
                if sq <= 7:
                    reg0 = par * 1024
                    ptdeps = [prev_pte_parity[par]]
                else:
                    reg0 = 0
                    ptdeps = [prev_pte_parity[0], prev_pte_parity[1]]
                ev_pt = sch.add("tensor",
                                (lambda sq_, par_, r0_: lambda t: [
                                    t.transpose(ps_pt[:, r0_ + mc * 128: r0_ + (mc + 1) * 128],
                                                P_bufs[par_][:, mc * 128:(mc + 1) * 128], identb)
                                    for mc in range(sq_ + 1)][-1])(sq, par, reg0),
                                after=[p2_evs[sq], ev_id] + ptdeps, sig=True)
                prev_pt_parity[par] = ev_pt
                if sq % 2 == 0:
                    ev_pte = sch.add("vector",
                                     (lambda kmax_, par_, r0_: lambda v: v.tensor_copy(
                                         PT_bufs[par_][:, 0:kmax_],
                                         ps_pt[:, r0_: r0_ + kmax_]))(kmax, par, reg0),
                                     after=[ev_pt, prev_zmm_parity[0], prev_zmm_parity[1]],
                                     sig=True)
                else:
                    ev_pte = sch.add("scalar",
                                     (lambda kmax_, par_, r0_: lambda s: s.copy(
                                         PT_bufs[par_][:, 0:kmax_],
                                         ps_pt[:, r0_: r0_ + kmax_]))(kmax, par, reg0),
                                     after=[ev_pt, prev_zmm_parity[0], prev_zmm_parity[1]],
                                     sig=True)
                prev_pte_parity[par] = ev_pte
                slot = sq % 2
                zreg = (ps_z[:, 0:128] if slot == 0 else ps_ms[:, 256:384])
                ev_zmm = sch.add("tensor",
                                 (lambda sq_, par_, zr: lambda t: [
                                     t.matmul(zr, V_tm[:, mc * 128:(mc + 1) * 128],
                                              PT_bufs[par_][:, mc * 128:(mc + 1) * 128],
                                              start=(mc == 0), stop=(mc == sq_))
                                     for mc in range(sq_ + 1)][-1])(sq, par, zreg),
                                 after=[ev_pte] + prev_zev_slot[slot], sig=True)
                prev_zmm_parity[par] = ev_zmm
                ev_zev = sch.add("vector",
                                 (lambda h2, sq_, zr: lambda v: v.tensor_copy(
                                     A1[:, sq_ * 2048 + h2 * 128: sq_ * 2048 + (h2 + 1) * 128],
                                     zr))(h_, sq, zreg),
                                 after=[ev_zmm], sig=True)
                prev_zev_slot[slot] = [ev_zev]
                prev_zev_any[0] = ev_zev
                z_done[sq].append(ev_zev)

            for i in range(NSC + 2):
                if 0 <= i - 1 < NSC:
                    emit_softmax(i - 1)
                if 0 <= i - 2 < NSC:
                    emit_tail(i - 2, h)
                if i < NSC:
                    emit_scores(i)
                drive_prepass(2)


        def drive_until_group(g):
            while prep_left[0] and g not in prep["grp"]:
                drive_prepass(1)
        while prep_left[0] and prep.get("s23") is None:
            drive_prepass(1)
        ev_t1 = prep["t"][1]
        ev_s23 = prep["s23"]

        prev_zev = prev_zev_any[0]
        prev_p2 = p2_evs[NSC - 1]
        prev_ptev = prev_pte_parity[1]
        ph1_done = [prev_zev, prev_p2, prev_pte_parity[0], prev_pte_parity[1],
                    prev_p1_half[0], prev_p1_half[1], prev_pt_parity[0], prev_pt_parity[1],
                    kqv_evts[0], kqv_evts[1]]

        if debug:
            dbg_z = nc.dram_tensor("dbg_z", [128, 32768], BF16, kind="ExternalOutput").ap()
            dbg_k = nc.dram_tensor("dbg_k", [128, 2048], BF16, kind="ExternalOutput").ap()
            dbg_q = nc.dram_tensor("dbg_q", [128, 2048], BF16, kind="ExternalOutput").ap()
            dbg_v = nc.dram_tensor("dbg_v", [128, 2048], BF16, kind="ExternalOutput").ap()
            dbg_p = nc.dram_tensor("dbg_p", [128, 2048], BF16, kind="ExternalOutput").ap()
            dbg_pt = nc.dram_tensor("dbg_pt", [128, 2048], BF16, kind="ExternalOutput").ap()
            dbg_st = nc.dram_tensor("dbg_st", [128, 640], F32, kind="ExternalOutput").ap()
            evs = []
            for nm, dst, srcb in (("z", dbg_z, A1[:, :]), ("k", dbg_k, K_sb), ("q", dbg_q, Q_sb),
                                  ("v", dbg_v, V_tm), ("p", dbg_p, P_bf), ("pt", dbg_pt, PT_sb),
                                  ("st", dbg_st, smalls[:, :])):
                evs.append(sch.add("gpsimd",
                                   (lambda d, s_: lambda g: g.dma_start(d, s_))(dst, srcb),
                                   after=ph1_done, dma=True))
            ph1_done = ph1_done + evs

        woT_tr = []
        for h in range(H):
            src = wo_in[h]
            for hf in range(2):
                par = (2 * h + hf) % 2
                raw = a3(RAW0 + par * 1024, 1024)
                src_ap = bass.AP(src.tensor, src.offset + hf * 8 * 16384,
                                 [[128, 128], [16384, 8], [1, 128]])
                ev_c = sch.add("gpsimd",
                               (lambda r, sap: lambda g: g.dma_start(
                                   r.rearrange("p (c f) -> p c f", c=8), sap))(raw, src_ap),
                               after=[raw_free[par]], dma=True)
                ev_t = sch.add("scalar",
                               (lambda h_, r, hf_: lambda s: s.dma_start(
                                   a2(h_ * 2048 + hf_ * 1024, 1024).rearrange(
                                       "p (c f) -> p c f", c=8), r,
                                   transpose=True))(h, raw, hf),
                               after=[ev_c, kqv_evts[0], kqv_evts[1]], dma=True)
                raw_free[par] = ev_t
                woT_tr.append(ev_t)

        X320, R0, SQD0, YTM0 = 0, 8192, 16384, 20480
        prev_x32 = [None, None]
        prev_r_half = [None, None]
        y_free = [[None, None], [None, None]]
        prev_r = None
        y_arena_ev = {}
        y32_ev = {}
        ln1_sq_last = None
        ln1_y_last = None
        last_amm = None
        for sc in range(NSC):
            par = sc % 2
            x32 = a3f(X320 + par * 4096, 2048)
            r_sb = a3f(R0 + par * 4096, 2048)
            sqd = x32
            y_tm = a3f(YTM0 + par * 4096, 2048)
            ls = ST_LN + 8 * par
            ev_x = sch.add("sync",
                           (lambda b, s0: lambda y: y.dma_start(b, x_in[s0:s0 + 128, :]))(
                               x32, sc * 128),
                           after=[prev_x32[par]] + (ph1_done if sc < 2 else []), dma=True)

            def attn_mm(sc_, hf_):
                def fn(t):
                    inst = None
                    for h in range(H):
                        for q4 in (2 * hf_, 2 * hf_ + 1):
                            inst = t.matmul(ps_sc[:, q4 * 512:(q4 + 1) * 512],
                                            A1[:, sc_ * 2048 + h * 128: sc_ * 2048 + (h + 1) * 128],
                                            a2(h * 2048 + q4 * 512, 512),
                                            start=(h == 0), stop=(h == 15))
                    return inst
                return fn
            r_evs = []
            for hf in (0, 1):
                ev_amm = sch.add("tensor", attn_mm(sc, hf),
                                 after=[prev_r_half[hf], prev_p1_half[0], prev_p1_half[1]] +
                                       z_done[sc] + woT_tr, sig=True)
                ev_rh = sch.add("vector",
                                (lambda r_, x_, hf2: lambda v: v.tensor_add(
                                    r_.bitcast(F32) if False else r_,
                                    ps_sc[:, hf2 * 1024:(hf2 + 1) * 1024], x_))(
                                    a3f(R0 + par * 4096 + hf * 2048, 1024),
                                    a3f(X320 + par * 4096 + hf * 2048, 1024), hf),
                                after=[ev_amm, ev_x], sig=True)
                prev_r_half[hf] = ev_rh
                r_evs.append(ev_rh)
            last_amm = ev_amm
            ev_r = r_evs[1]
            prev_r = ev_r
            ev_l1 = sch.add("vector",
                            (lambda r_, l_: lambda v: v.reduce_sum(stc(l_), r_, axis=AX.X))(r_sb, ls),
                            after=r_evs, sig=True)
            ev_l2 = sch.add("vector",
                            (lambda l_: lambda v: v.tensor_scalar_mul(stc(l_ + 1), stc(l_),
                                                                      -1.0 / D))(ls),
                            after=[ev_l1], sig=True, fence=True)
            ev_l3 = sch.add("scalar",
                            (lambda r_, q_, l_: lambda s: s.activation(q_, r_, AF.Square,
                                                                       bias=stc(l_ + 1),
                                                                       accum_out=stc(l_ + 2)))(
                                r_sb, sqd, ls),
                            after=[ev_l2, ln1_sq_last], sig=True)
            ev_l4 = ev_l3
            ln1_sq_last = ev_l3
            ev_l5 = sch.add("vector",
                            (lambda l_: lambda v: v.tensor_scalar(stc(l_ + 3), stc(l_ + 2),
                                                                  1.0 / D, EPS, ALU.mult, ALU.add))(ls),
                            after=[ev_l4], sig=True, fence=True)
            ev_l6 = sch.add("scalar",
                            (lambda l_: lambda s: s.activation(stc(l_ + 4), stc(l_ + 3), AF.Ln))(ls),
                            after=[ev_l5], sig=True)
            ev_l7 = sch.add("scalar",
                            (lambda l_: lambda s: s.activation(stc(l_ + 5), stc(l_ + 4), AF.Exp,
                                                               scale=-0.5))(ls),
                            after=[ev_l6], sig=True, fence=True)
            ev_l8 = sch.add("vector",
                            (lambda l_: lambda v: v.tensor_mul(stc(l_ + 6), stc(l_ + 1),
                                                               stc(l_ + 5)))(ls),
                            after=[ev_l7], sig=True)
            ev_y = sch.add("scalar",
                           (lambda r_, y_, l_: lambda s: s.activation(y_, r_, AF.Identity,
                                                                      scale=stc(l_ + 5),
                                                                      bias=stc(l_ + 6)))(
                               r_sb, y_tm, ls),
                           after=[ev_l8] + y_free[par], sig=True)
            ln1_y_last = ev_y
            ev_yh = sch.add("vector",
                            (lambda y_, sc_: lambda v: v.tensor_copy(
                                A1[:, sc_ * 2048:(sc_ + 1) * 2048], y_))(y_tm, sc),
                            after=[ev_y], sig=True)
            y_arena_ev[sc] = ev_yh
            ev_ysp = sch.add("gpsimd",
                             (lambda y_, s0: lambda g: g.dma_start(y32_s[s0:s0 + 128, :], y_))(
                                 y_tm, sc * 128),
                             after=[ev_y], dma=True)
            y32_ev[sc] = ev_ysp
            y_free[par] = [ev_yh, ev_ysp]
            prev_x32[par] = ev_l3
            drive_prepass(45)

        ph2_done = [last_amm, prev_r, ln1_y_last, y_free[0][0], y_free[0][1],
                    y_free[1][0], y_free[1][1]]

        YFM0, FTM0, W2R0, SIL0, GR0, OUT0 = 0, 8192, 16384, 24576, 26624, 28672
        FACC0, Y32R0, W1B0, W3B0 = 0, 8192, 20480, 24576

        f_acc = A3[:, FACC0:FACC0 + 8192].bitcast(F16)
        prev_y32r = [None, None]
        prev_w1b = [None, None]
        prev_w3b = [None, None]
        prev_w2r = [None, None, None, None]
        prev_gr = [None, None, None, None]
        prev_sil = [None, None]
        prev_h1 = None
        prev_h3 = None
        prev_fa_slot = [None, None, None, None]
        prev_facc_free = None
        ln2_sq_last = None
        prev_out_ev = None
        prev_yfm = None
        prev_ftm = None
        first3 = True

        for b in range(4):
            sc0 = b * 4
            yfm_evs = []
            for k in range(4):
                sc = sc0 + k
                in_ap = A1[:, sc * 2048:(sc + 1) * 2048]
                out_ap = bass.AP(A2, YFM0 + k * 128, [[32768, 128], [512, 16], [1, 128]])
                ev = sch.add("scalar",
                             (lambda o, i: lambda s: s.dma_start(o, i, transpose=True))(out_ap, in_ap),
                             after=[y_arena_ev[sc], prev_yfm] + (ph2_done if first3 else []),
                             dma=True)
                yfm_evs.append(ev)
            prev_yfm = yfm_evs[-1]
            yfm_last = yfm_evs[-1]
            ev_fz = sch.add("vector", lambda v: v.memset(f_acc[:, :], 0.0),
                            after=[prev_facc_free] + (ph2_done if first3 else []), sig=True)

            if b == 0:
                while prep_left[0]:
                    drive_prepass(50)
                prep_all = [prep["grp"][g][-1] for g in prep["grp"]] + [pbe[0], pbe[1]]
            groups = [list(range(i, min(i + 4, NFC))) for i in range(0, NFC, 4)]
            for gi_, grp in enumerate(groups):
                grp_deps = prep_all if b == 0 else []
                g_ready = {}
                for fc in grp:
                    slot = fc % 4
                    parw = fc % 2
                    w1b = a3(W1B0 + parw * 2048, 2048)
                    w3b = a3(W3B0 + parw * 2048, 2048)
                    w2b = a2(W2R0 + slot * 2048, 2048)
                    silu_t = a2f(SIL0 + parw * 1024, 512)
                    gbuf = a2(GR0 + slot * 512, 512)
                    ev_w1 = sch.add("sync",
                                    (lambda w, fc_: lambda y: y.dma_start(w, w1qT_s[fc_]))(w1b, fc),
                                    after=grp_deps + [prev_w1b[parw]] +
                                          (ph2_done if first3 else []), dma=True)
                    ev_w3 = sch.add("sync",
                                    (lambda w, fc_: lambda y: y.dma_start(w, w3qT_s[fc_]))(w3b, fc),
                                    after=grp_deps + [prev_w3b[parw]], dma=True)
                    ev_w2 = sch.add("gpsimd",
                                    (lambda w, fc_: lambda g: g.dma_start(w, w2qT_s[fc_]))(w2b, fc),
                                    after=grp_deps + [prev_w2r[slot]] +
                                          (ph2_done if first3 else []), dma=True)
                    first3 = False

                    def h_mm(wb, ps_ap):
                        def fn(t):
                            inst = None
                            for dc in range(16):
                                inst = t.matmul(ps_ap, wb[:, dc * 128:(dc + 1) * 128],
                                                a2(YFM0 + dc * 512, 512),
                                                start=(dc == 0), stop=(dc == 15))
                            return inst
                        return fn
                    ev_h1 = sch.add("tensor", h_mm(w1b, ps_z[:, 0:512]),
                                    after=[ev_w1, yfm_last, prev_h1,
                                           (prev_zev if b == 0 else None)], sig=True)
                    prev_w1b[parw] = ev_h1
                    ev_sl = sch.add("scalar",
                                    (lambda st_: lambda s: s.activation(st_, ps_z[:, 0:512],
                                                                        AF.Silu,
                                                                        scale=stc(ST_THR[1])))(silu_t),
                                    after=[ev_h1, ev_t1, prev_sil[parw]], sig=True)
                    prev_h1 = ev_sl
                    ev_h3 = sch.add("tensor", h_mm(w3b, ps_h3),
                                    after=[ev_w3, yfm_last, prev_h3,
                                           (prev_ptev if b == 0 else None),
                                           (prev_zev_any[0] if b == 0 else None),
                                           (prep["t"][2] if b == 0 else None)], sig=True)
                    prev_w3b[parw] = ev_h3
                    ev_g = sch.add("vector",
                                   (lambda gb, st_: lambda v: v.tensor_tensor(gb, st_, ps_h3,
                                                                              ALU.mult))(
                                       gbuf, silu_t),
                                   after=[ev_h3, ev_sl, prev_gr[slot]], sig=True)
                    prev_h3 = ev_g
                    prev_sil[parw] = ev_g
                    g_ready[fc] = (ev_g, ev_w2)

                for dc in range(16):
                    slot4 = dc % 4

                    def f_mm(grp_, dc_):
                        def fn(t):
                            inst = None
                            for gi, fc in enumerate(grp_):
                                sl = fc % 4
                                inst = t.matmul(ps_sc[:, (dc_ % 4) * 512:(dc_ % 4) * 512 + 512],
                                                a2(W2R0 + sl * 2048, 2048)[:, dc_ * 128:(dc_ + 1) * 128],
                                                a2(GR0 + sl * 512, 512),
                                                start=(gi == 0), stop=(gi == len(grp_) - 1))
                            return inst
                        return fn
                    deps = [g_ready[fc][0] for fc in grp] + [g_ready[fc][1] for fc in grp]
                    ev_f = sch.add("tensor", f_mm(grp, dc),
                                   after=deps + [prev_fa_slot[slot4],
                                                 (prev_r if b == 0 and dc < 4 else None)], sig=True)
                    ev_fa = sch.add("vector",
                                    (lambda dc_: lambda v: v.tensor_add(
                                        f_acc[:, dc_ * 512:(dc_ + 1) * 512],
                                        f_acc[:, dc_ * 512:(dc_ + 1) * 512],
                                        ps_sc[:, (dc_ % 4) * 512:(dc_ % 4) * 512 + 512]))(dc),
                                    after=[ev_f, ev_fz], sig=True)
                    prev_fa_slot[slot4] = ev_fa
                last_fa = prev_fa_slot[15 % 4]
                for fc in grp:
                    prev_w2r[fc % 4] = last_fa
                    prev_gr[fc % 4] = last_fa
            facc_final = [e for e in prev_fa_slot if e is not None]

            ftm_evs = []
            for dc in range(16):
                in_ap = A3[:, FACC0 + dc * 512:FACC0 + (dc + 1) * 512].bitcast(F16)
                out_ap = bass.AP(A2, FTM0 + dc * 128, [[32768, 128], [2048, 4], [1, 128]]).bitcast(F16)
                ev = sch.add("scalar",
                             (lambda o, i: lambda s: s.dma_start(o, i, transpose=True))(out_ap, in_ap),
                             after=facc_final + [prev_ftm], dma=True)
                ftm_evs.append(ev)
            prev_ftm = ftm_evs[-1]
            prev_facc_free = ftm_evs[-1]

            for k in range(4):
                sc = sc0 + k
                par = sc % 2
                y32b = a3f(Y32R0 + par * 4096, 2048)
                outst = a2f(OUT0, 2048)
                sqd4 = outst
                ls = ST_LN + 8 * par
                ev_yl = sch.add("sync",
                                (lambda b_, s0: lambda y: y.dma_start(b_, y32_s[s0:s0 + 128, :]))(
                                    y32b, sc * 128),
                                after=[y32_ev[sc], prev_y32r[par]] +
                                      (ph2_done if b == 0 and k < 2 else []), dma=True)
                ftm_sc = A2[:, FTM0 + k * 2048:FTM0 + (k + 1) * 2048].bitcast(F16)
                ev_r2 = sch.add("vector",
                                (lambda y_, f_: lambda v: v.scalar_tensor_tensor(
                                    y_, f_, stc(ST_S23), y_, ALU.mult, ALU.add))(y32b, ftm_sc),
                                after=[ev_yl, ftm_evs[-1], ev_s23], sig=True)
                prev_y32r[par] = ev_r2
                ev_m1 = sch.add("vector",
                                (lambda y_, l_: lambda v: v.reduce_sum(stc(l_), y_, axis=AX.X))(
                                    y32b, ls),
                                after=[ev_r2], sig=True, fence=True)
                ev_m2 = sch.add("vector",
                                (lambda l_: lambda v: v.tensor_scalar_mul(stc(l_ + 1), stc(l_),
                                                                          -1.0 / D))(ls),
                                after=[ev_m1], sig=True, fence=True)
                ev_m3 = sch.add("scalar",
                                (lambda r_, q_, l_: lambda s: s.activation(q_, r_, AF.Square,
                                                                           bias=stc(l_ + 1),
                                                                           accum_out=stc(l_ + 2)))(
                                    y32b, sqd4, ls),
                                after=[ev_m2, ln2_sq_last, prev_out_ev], sig=True)
                ev_m4 = ev_m3
                ln2_sq_last = ev_m3
                ev_m5 = sch.add("vector",
                                (lambda l_: lambda v: v.tensor_scalar(stc(l_ + 3), stc(l_ + 2),
                                                                      1.0 / D, EPS, ALU.mult,
                                                                      ALU.add))(ls),
                                after=[ev_m4], sig=True, fence=True)
                ev_m6 = sch.add("scalar",
                                (lambda l_: lambda s: s.activation(stc(l_ + 4), stc(l_ + 3),
                                                                   AF.Ln))(ls),
                                after=[ev_m5], sig=True)
                ev_m7 = sch.add("scalar",
                                (lambda l_: lambda s: s.activation(stc(l_ + 5), stc(l_ + 4),
                                                                   AF.Exp, scale=-0.5))(ls),
                                after=[ev_m6], sig=True, fence=True)
                ev_m8 = sch.add("vector",
                                (lambda l_: lambda v: v.tensor_mul(stc(l_ + 6), stc(l_ + 1),
                                                                   stc(l_ + 5)))(ls),
                                after=[ev_m7], sig=True)
                ev_o = sch.add("scalar",
                               (lambda r_, o_, l_: lambda s: s.activation(o_, r_, AF.Identity,
                                                                          scale=stc(l_ + 5),
                                                                          bias=stc(l_ + 6)))(
                                   y32b, outst, ls),
                               after=[ev_m8, prev_out_ev], sig=True)
                ev_st = sch.add("sync",
                                (lambda o_, s0: lambda y: y.dma_start(out_hbm[s0:s0 + 128, :], o_))(
                                    outst, sc * 128),
                                after=[ev_o], dma=True)
                prev_out_ev = ev_st

        if debug:
            dbg_a2 = nc.dram_tensor("dbg_a2", [128, 32768], BF16, kind="ExternalOutput").ap()
            dbg_a3 = nc.dram_tensor("dbg_a3", [128, 28672], BF16, kind="ExternalOutput").ap()
            sch.add("gpsimd", lambda g: g.dma_start(dbg_a2, A2[:, :]),
                    after=[prev_out_ev, prev_ftm, prev_facc_free], dma=True)
            sch.add("gpsimd", lambda g: g.dma_start(dbg_a3, A3[:, :]),
                    after=[prev_out_ev], dma=True)
        final_val = sch.counts["d_y"]
        final_valg = sch.counts["d_g"]
        sch.ops.append(("sync", lambda y: None, [("d_y", final_val), ("d_g", final_valg)], None, False))

        sch.check()

        with nc.Block() as block:
            def emit_engine(decor, ename):
                ops = [(f, w, i, d) for (ee, f, w, i, d) in sch.ops if ee == ename]

                @decor
                def _(e):
                    for fn, waits, inc, dma in ops:
                        for s_, v_ in waits:
                            e.wait_ge(sems[s_], v_)
                        inst = fn(e)
                        if inc is not None:
                            s_, v_ = inc
                            if dma:
                                inst.then_inc(sems[s_], 16)
                            else:
                                if inst is None:
                                    inst = e.nop()
                                inst.then_inc(sems[s_], 1)

            emit_engine(block.tensor, "tensor")
            emit_engine(block.vector, "vector")
            emit_engine(block.scalar, "scalar")
            emit_engine(block.gpsimd, "gpsimd")
            emit_engine(block.sync, "sync")

    return nc


_NC_CACHE = None


def kernel(L0_attn_x, L0_attn_W_Q, L0_attn_W_K, L0_attn_W_V, L0_attn_W_O,
           L0_ffn_w1, L0_ffn_w3, L0_ffn_w2):
    global _NC_CACHE
    x = np.ascontiguousarray(np.asarray(L0_attn_x, dtype=np.float32))
    ws = {
        "wq": np.ascontiguousarray(np.asarray(L0_attn_W_Q, dtype=np.float32)),
        "wk": np.ascontiguousarray(np.asarray(L0_attn_W_K, dtype=np.float32)),
        "wv": np.ascontiguousarray(np.asarray(L0_attn_W_V, dtype=np.float32)),
        "wo": np.ascontiguousarray(np.asarray(L0_attn_W_O, dtype=np.float32)),
        "w1": np.ascontiguousarray(np.asarray(L0_ffn_w1, dtype=np.float32)),
        "w3": np.ascontiguousarray(np.asarray(L0_ffn_w3, dtype=np.float32)),
        "w2": np.ascontiguousarray(np.asarray(L0_ffn_w2, dtype=np.float32)),
    }
    if _NC_CACHE is None:
        _NC_CACHE = build_nc()
    in_maps = [dict(x=np.ascontiguousarray(x[b]), **ws) for b in range(8)]
    res = run_bass_kernel_spmd(_NC_CACHE, in_maps, core_ids=list(range(8)))
    return np.stack([res.results[b]["out"] for b in range(8)], axis=0)


# version 36 (final)
# speedup vs baseline: 1.0249x; 1.0249x over previous
import numpy as np
from contextlib import ExitStack

import concourse.bass as bass
import concourse.mybir as mybir
from concourse.bass_utils import run_bass_kernel_spmd

F32 = mybir.dt.float32
BF16 = mybir.dt.bfloat16
F16 = mybir.dt.float16
AF = mybir.ActivationFunctionType
AX = mybir.AxisListType
ALU = mybir.AluOpType

S = 2048
D = 2048
H = 16
E = 128
FH = 5461
FP = 5504
NFC = 43
NSC = 16
QSCALE = 1.0 / float(np.sqrt(128.0))
SMAX = 24.0
EPS = 1e-5

ENGINES = ("tensor", "vector", "scalar", "gpsimd", "sync")


class Sched:
    def __init__(self):
        self.ops = []
        self.counts = {}
        self.seen = {e: {} for e in ENGINES}
        self.esem = {"tensor": "e_t", "vector": "e_v", "scalar": "e_s",
                     "gpsimd": "e_g", "sync": "e_y"}
        self.dsem = {"sync": "d_y", "scalar": "d_s", "gpsimd": "d_g"}

    def add(self, engine, fn, after=(), sig=False, dma=False, fence=False):
        waits = []
        for ev in after:
            if ev is None:
                continue
            sem, val = ev
            if sem == self.esem[engine] and not fence:
                continue
            prev = self.seen[engine].get(sem, -1)
            if val > prev:
                waits.append((sem, val))
                self.seen[engine][sem] = val
        ev_out = None
        if dma:
            sem = self.dsem[engine]
            self.counts[sem] = self.counts.get(sem, 0) + 16
            ev_out = (sem, self.counts[sem])
        elif sig:
            sem = self.esem[engine]
            self.counts[sem] = self.counts.get(sem, 0) + 1
            ev_out = (sem, self.counts[sem])
        self.ops.append((engine, fn, waits, ev_out, dma))
        return ev_out

    def check(self):
        per_eng = {e: [op for op in self.ops if op[0] == e] for e in ENGINES}
        idx = {e: 0 for e in ENGINES}
        counts = {}
        total = len(self.ops)
        done = 0
        while done < total:
            progress = False
            for e in ENGINES:
                while idx[e] < len(per_eng[e]):
                    _, _, waits, inc, _ = per_eng[e][idx[e]]
                    if all(counts.get(s_, 0) >= v_ for s_, v_ in waits):
                        if inc is not None:
                            s_, v_ = inc
                            counts[s_] = max(counts.get(s_, 0), v_)
                        idx[e] += 1
                        done += 1
                        progress = True
                    else:
                        break
            if not progress:
                stuck = {e: (per_eng[e][idx[e]][2] if idx[e] < len(per_eng[e]) else "done")
                         for e in ENGINES}
                raise RuntimeError(f"deadlock: {stuck}")


def build_nc(debug=False):
    nc = bass.Bass("TRN2", target_bir_lowering=False)
    KIND = "ExternalOutput" if debug else "Internal"

    x_in = nc.dram_tensor("x", [S, D], F32, kind="ExternalInput").ap()
    wq_in = nc.dram_tensor("wq", [H, E, D], F32, kind="ExternalInput").ap()
    wk_in = nc.dram_tensor("wk", [H, E, D], F32, kind="ExternalInput").ap()
    wv_in = nc.dram_tensor("wv", [H, E, D], F32, kind="ExternalInput").ap()
    wo_in = nc.dram_tensor("wo", [H, D, E], F32, kind="ExternalInput").ap()
    w1_in = nc.dram_tensor("w1", [FH, D], F32, kind="ExternalInput").ap()
    w3_in = nc.dram_tensor("w3", [FH, D], F32, kind="ExternalInput").ap()
    w2_in = nc.dram_tensor("w2", [D, FH], F32, kind="ExternalInput").ap()
    out_hbm = nc.dram_tensor("out", [S, D], F32, kind="ExternalOutput").ap()

    w1qT_s = nc.dram_tensor("w1qT_s", [NFC, 128, D], BF16, kind=KIND).ap()
    w3qT_s = nc.dram_tensor("w3qT_s", [NFC, 128, D], BF16, kind=KIND).ap()
    w2qT_s = nc.dram_tensor("w2qT_s", [NFC, 128, D], BF16, kind=KIND).ap()
    y32_s = nc.dram_tensor("y32_s", [S, D], F32, kind=KIND).ap()

    es = ExitStack()
    sch = Sched()

    with es:
        A1 = es.enter_context(nc.sbuf_tensor("A1", [128, 32768], BF16))
        A2 = es.enter_context(nc.sbuf_tensor("A2", [128, 32768], BF16))
        A3 = es.enter_context(nc.sbuf_tensor("A3", [128, 28672], BF16))
        smalls = es.enter_context(nc.sbuf_tensor("smalls", [128, 640], F32))
        smallsb = es.enter_context(nc.sbuf_tensor("smallsb", [128, 256], BF16))

        def a3(c0, n):
            return A3[:, c0:c0 + n]

        def a3f(c0, nf):
            return A3[:, c0:c0 + 2 * nf].bitcast(F32)

        def a2(c0, n):
            return A2[:, c0:c0 + n]

        def a2f(c0, nf):
            return A2[:, c0:c0 + 2 * nf].bitcast(F32)

        ones128 = smalls[:, 0:128]
        maskdiag = smalls[:, 128:256]
        stat = smalls[:, 256:512]
        iota_p = smalls[:, 512:640]
        identb = smallsb[:, 0:128]
        mask_bf = smallsb[:, 128:256]

        ST_NEGSMAX = 0
        ST_THR = {1: 1, 3: 2, 2: 3}
        ST_NEGTHR = {1: 4, 3: 5, 2: 6}
        ST_S23 = 7
        ST_SM = 8
        ST_LN = 24
        ST_ABS = 44
        ST_TOT = 85
        ST_SF = {1: 89, 3: 90, 2: 91}

        def stc(i, n=1):
            return stat[:, i:i + n]

        ps_sc = es.enter_context(nc.psum_tensor("ps_sc", [128, 2048], F32))
        ps_pt = es.enter_context(nc.psum_tensor("ps_pt", [128, 2048], BF16))
        ps_z = es.enter_context(nc.psum_tensor("ps_z", [128, 512], F32))
        ps_ms = es.enter_context(nc.psum_tensor("ps_ms", [128, 512], F32))
        ps_h3 = ps_ms[:, 0:512]

        sems = {}
        for s_name in ("e_t", "e_v", "e_s", "e_g", "e_y", "d_y", "d_s", "d_g"):
            sems[s_name] = es.enter_context(nc.semaphore(s_name))

        ev_if = sch.add("gpsimd", lambda g: g.iota(maskdiag, [[1, 128]], channel_multiplier=0,
                                                   allow_small_or_imprecise_dtypes=True), sig=True)
        ev_ip = sch.add("gpsimd", lambda g: g.iota(iota_p, [[0, 128]], channel_multiplier=1,
                                                   allow_small_or_imprecise_dtypes=True), sig=True)
        ev_id = sch.add("vector", lambda v: v.tensor_tensor(identb, maskdiag, iota_p, ALU.is_equal),
                        after=[ev_if, ev_ip], sig=True)
        ev_mk0 = sch.add("vector", lambda v: v.tensor_tensor(maskdiag, maskdiag, iota_p, ALU.is_gt),
                         after=[ev_id], sig=True, fence=True)
        ev_mask = sch.add("vector", lambda v: v.tensor_scalar_mul(maskdiag, maskdiag, -1.0e9),
                          after=[ev_mk0], sig=True, fence=True)
        ev_maskb = sch.add("vector", lambda v: v.tensor_copy(mask_bf, maskdiag),
                           after=[ev_mask], sig=True, fence=True)
        sch.add("vector", lambda v: v.memset(ones128, 1.0))
        sch.add("vector", lambda v: v.memset(stc(ST_TOT, 3), 0.0))
        ev_cst = sch.add("vector", lambda v: v.memset(stc(ST_NEGSMAX), -float(SMAX)), sig=True)

        RAW0 = 22528
        x_done = []
        prev_xtr = [None, None]
        for sc in range(NSC):
            par = sc % 2
            buf = a3(RAW0 + par * 2048, 2048)
            ev_ld = sch.add("gpsimd",
                            (lambda b, s0: lambda g: g.dma_start(b, x_in[s0:s0 + 128, :]))(buf, sc * 128),
                            after=[prev_xtr[par]], dma=True)
            out_ap = bass.AP(A2, sc * 128, [[32768, 128], [2048, 16], [1, 128]])
            ev_tr = sch.add("scalar",
                            (lambda o, b: lambda s: s.dma_start(o, b, transpose=True))(out_ap, buf),
                            after=[ev_ld], dma=True)
            prev_xtr[par] = ev_tr
            x_done.append(ev_tr)
        x_ready = x_done[-1]

        QIN0, QOUT0, TB0 = 16384, 24576, 26624
        PBF1 = 20480
        prev_zev_slot = [[], []]
        pbe = [prev_xtr[0], prev_xtr[1]]
        pbe = [None, None]
        prep = {"t": {}, "n": {}, "q": {}, "s23": None}

        def prepass_jobs():
            prev_ac = None
            prev_thr = None
            k_abs = [0]

            def abs_tiles(w_ap, nrows, ncols, tag):
                nonlocal prev_ac
                for rt in range((nrows + 127) // 128):
                    r0 = rt * 128
                    rn = min(128, nrows - r0)
                    for ct in range((ncols + 1023) // 1024):
                        c0 = ct * 1024
                        cn = min(1024, ncols - c0)
                        par = k_abs[0] % 2
                        buf = a3f(QIN0 + par * 2048, 1024)
                        ev_ld = sch.add("sync",
                                        (lambda b, r0_, rn_, c0_, cn_: lambda y: y.dma_start(
                                            b[0:rn_, 0:cn_], w_ap[r0_:r0_ + rn_, c0_:c0_ + cn_]))(
                                            buf, r0, rn, c0, cn),
                                        after=[pbe[par]], dma=True)
                        col = ST_ABS + (k_abs[0] % 40)
                        ev_rd = sch.add("vector",
                                        (lambda b, rn_, cn_, cl: lambda v: v.reduce_sum(
                                            stc(cl)[0:rn_, :], b[0:rn_, 0:cn_], axis=AX.X,
                                            apply_absolute_value=True))(buf, rn, cn, col),
                                        after=[ev_ld], sig=True)
                        ev_ac = sch.add("vector",
                                        (lambda cl, rn_, tg: lambda v: v.tensor_add(
                                            stc(ST_TOT + tg)[0:rn_, :], stc(ST_TOT + tg)[0:rn_, :],
                                            stc(cl)[0:rn_, :]))(col, rn, tag),
                                        after=[ev_rd, prev_ac], sig=True, fence=True)
                        pbe[par] = ev_rd
                        prev_ac = ev_ac
                        k_abs[0] += 1
                        yield
                return

            def thr_chain(tag, m, total):
                nonlocal prev_thr, prev_ac
                ev_mm = sch.add("tensor",
                                (lambda tg: lambda t: t.matmul(ps_ms[:, 0:1], ones128,
                                                               stc(ST_TOT + tg), start=True,
                                                               stop=True))(tag),
                                after=[prev_ac, ev_cst, prev_thr] + prev_zev_slot[1], sig=True)
                ev_t = sch.add("scalar",
                               (lambda m_, sc_: lambda s: s.activation(stc(ST_THR[m_]),
                                                                       ps_ms[:, 0:1],
                                                                       AF.Identity, scale=sc_))(
                                   m, 0.5 / float(total)),
                               after=[ev_mm], sig=True)
                ev_n = sch.add("vector",
                               (lambda m_: lambda v: v.tensor_scalar_mul(stc(ST_NEGTHR[m_]),
                                                                         stc(ST_THR[m_]), -1.0))(m),
                               after=[ev_t], sig=True)
                ev_sf = sch.add("vector",
                                (lambda m_: lambda v: v.tensor_scalar_mul(stc(ST_SF[m_]),
                                                                          stc(ST_THR[m_]), 2.0))(m),
                                after=[ev_t], sig=True)
                prep["t"][m] = ev_t
                prep["n"][m] = ev_n
                prep["sf"] = prep.get("sf", {})
                prep["sf"][m] = ev_sf
                prev_zev_slot[1].append(ev_t)
                prev_thr = ev_t

            yield from abs_tiles(w1_in, FH, D, 0)
            thr_chain(0, 1, FH * D)
            yield
            yield from abs_tiles(w3_in, FH, D, 1)
            thr_chain(1, 3, FH * D)
            yield
            yield from abs_tiles(w2_in, D, FH, 2)
            thr_chain(2, 2, D * FH)
            prep["s23"] = sch.add("vector",
                                  lambda v: v.tensor_mul(stc(ST_S23), stc(ST_THR[2]),
                                                         stc(ST_THR[3])),
                                  after=[prep["t"][2], prep["t"][3]], sig=True, fence=True)
            yield

            prev_out = [None, None]
            prev_tb = [None, None]
            kq = [0]
            prep["grp"] = {}
            prep["qt"] = {}

            def quant_tile(w_ap, dst, nrows, ncols, m, kind, rt, ct, grp_evs):
                r0 = rt * 128
                rn = max(0, min(128, nrows - r0))
                rpad = 128
                c0 = ct * 1024
                cn = max(0, min(1024, ncols - c0))
                cpad = min(1024, (D if kind == "fd" else FP) - c0)
                thr_ev, nthr_ev = prep["t"][m], prep["n"][m]
                par = kq[0] % 2
                buf = a3f(QIN0 + par * 2048, 1024)
                obuf = a3(QOUT0 + par * 1024, 1024)
                tbuf = a3(TB0 + par * 1024, 1024)
                if rn > 0 and cn > 0:
                    ev_ld = sch.add("sync",
                                    (lambda b, r0_, rn_, c0_, cn_: lambda y: y.dma_start(
                                        b[0:rn_, 0:cn_], w_ap[r0_:r0_ + rn_, c0_:c0_ + cn_]))(
                                        buf, r0, rn, c0, cn),
                                    after=[pbe[par], thr_ev, nthr_ev, prev_out[par]], dma=True)
                    ev_pre = None
                    if rpad > rn or cpad > cn:
                        ev_pre = sch.add("vector",
                                         (lambda o, rp_, cp_: lambda v: v.memset(
                                             o[0:rp_, 0:cp_], 0.0))(obuf, rpad, cpad),
                                         after=[prev_out[par]], sig=True)
                    if kq[0] % 2 == 1:
                        ev_s1 = sch.add("vector",
                                        (lambda o, b, rn_, cn_, m_: lambda v: v.tensor_scalar(
                                            o[0:rn_, 0:cn_], b[0:rn_, 0:cn_],
                                            stc(ST_THR[m_])[0:rn_, :], 2.0, ALU.is_ge,
                                            ALU.mult))(obuf, buf, rn, cn, m),
                                        after=[ev_ld, ev_pre], sig=True, fence=True)
                        ev_s2 = sch.add("vector",
                                        (lambda b, rn_, cn_, m_: lambda v: v.tensor_scalar(
                                            b[0:rn_, 0:cn_], b[0:rn_, 0:cn_],
                                            stc(ST_NEGTHR[m_])[0:rn_, :], 2.0, ALU.is_le,
                                            ALU.mult))(buf, rn, cn, m),
                                        after=[ev_s1], sig=True, fence=True)
                        ev_q = sch.add("vector",
                                       (lambda o, b, rn_, cn_: lambda v: v.tensor_sub(
                                           o[0:rn_, 0:cn_], o[0:rn_, 0:cn_], b[0:rn_, 0:cn_]))(
                                           obuf, buf, rn, cn),
                                       after=[ev_s2], sig=True, fence=True)
                    else:
                        ev_s1 = sch.add("scalar",
                                        (lambda o, b, rn_, cn_, m_: lambda s: s.activation(
                                            o[0:rn_, 0:cn_], b[0:rn_, 0:cn_], AF.Sign,
                                            bias=stc(ST_NEGTHR[m_])[0:rn_, :]))(obuf, buf, rn, cn, m),
                                        after=[ev_ld, ev_pre], sig=True)
                        ev_s2 = sch.add("scalar",
                                        (lambda b, rn_, cn_, m_: lambda s: s.activation(
                                            b[0:rn_, 0:cn_], b[0:rn_, 0:cn_], AF.Sign,
                                            bias=stc(ST_THR[m_])[0:rn_, :]))(buf, rn, cn, m),
                                        after=[ev_s1], sig=True, fence=True)
                        ev_q = sch.add("vector",
                                       (lambda o, b, rn_, cn_: lambda v: v.tensor_add(
                                           o[0:rn_, 0:cn_], o[0:rn_, 0:cn_], b[0:rn_, 0:cn_]))(
                                           obuf, buf, rn, cn),
                                       after=[ev_s2], sig=True)
                    pbe[par] = ev_q if kq[0] % 2 == 1 else ev_s2
                else:
                    ev_q = sch.add("vector",
                                   (lambda o, rp_, cp_: lambda v: v.memset(
                                       o[0:rp_, 0:cp_], 0.0))(obuf, rpad, cpad),
                                   after=[prev_out[par]], sig=True)
                ncb = cpad // 128
                ev_tr = sch.add("scalar",
                                (lambda t_, o, cp_, nc_: lambda s: s.dma_start(
                                    t_[:, 0:cp_].rearrange("p (c f) -> p c f", c=nc_),
                                    o[0:128, 0:cp_], transpose=True))(tbuf, obuf, cpad, ncb),
                                after=[ev_q, prev_tb[par]], dma=True)
                prev_out[par] = ev_tr
                prep["qt"][m] = ev_tr
                if kind == "fd":
                    ev_sp = sch.add("gpsimd",
                                    (lambda d, t_, rt_, c0_, cp_: lambda g: g.dma_start(
                                        d[rt_][:, c0_:c0_ + cp_], t_[:, 0:cp_]))(
                                        dst, tbuf, rt, c0, cpad),
                                    after=[ev_tr], dma=True)
                else:
                    dst_ap = bass.AP(dst.tensor, dst.offset + (ct * 8) * 128 * D + rt * 128,
                                     [[D, 128], [128 * D, ncb], [1, 128]])
                    ev_sp = sch.add("gpsimd",
                                    (lambda da, t_, cp_, nc_: lambda g: g.dma_start(
                                        da, t_[:, 0:cp_].rearrange("p (c f) -> p c f", c=nc_)))(
                                        dst_ap, tbuf, cpad, ncb),
                                    after=[ev_tr], dma=True)
                prev_tb[par] = ev_sp
                grp_evs.append(ev_sp)
                kq[0] += 1

            for gg in range((NFC + 7) // 8):
                grp_evs = []
                fcs = list(range(gg * 8, min(gg * 8 + 8, NFC)))
                for fc in fcs:
                    for ct in range(2):
                        quant_tile(w1_in, w1qT_s, FH, D, 1, "fd", fc, ct, grp_evs)
                        yield
                for fc in fcs:
                    for ct in range(2):
                        quant_tile(w3_in, w3qT_s, FH, D, 3, "fd", fc, ct, grp_evs)
                        yield
                for rt in range(16):
                    quant_tile(w2_in, w2qT_s, D, FH, 2, "df", rt, gg, grp_evs)
                    yield
                prep["grp"][gg * 2] = grp_evs
                prep["grp"][gg * 2 + 1] = grp_evs


        prep_gen = prepass_jobs()
        prep_left = [True]

        def drive_prepass(n):
            for _ in range(n):
                try:
                    next(prep_gen)
                except StopIteration:
                    prep_left[0] = False
                    return

        WT0 = 0
        K_sb = a3(6144, 2048)
        Q_sb = a3(8192, 2048)
        V_tm = a3(10240, 2048)
        P_bf = a3(12288, 2048)
        V_fm = P_bf
        PT_sb = a3(14336, 2048)

        z_done = {sc: [] for sc in range(NSC)}
        kqv_evts = [None, None]
        prev_p1_half = [None, None]
        prev_pt_parity = [None, None]
        prev_pte_parity = [None, None]
        prev_zmm_parity = [None, None]
        prev_zev_any = [None]
        jcnt = [0]
        prev_kev = None
        prev_qev = None
        prev_vev = None
        prev_vtr = None
        raw_free = [prev_xtr[0], prev_xtr[1]]
        wt_tr = {}

        kqv_free = {"k": None, "q": None, "v": None}

        def emit_w_load(h, after_extra=()):
            for i, (m, src) in enumerate((("k", wk_in[h]), ("q", wq_in[h]), ("v", wv_in[h]))):
                for hf in range(2):
                    par = (2 * (3 * h + i) + hf) % 2
                    raw = a3(RAW0 + par * 1024, 1024)
                    ev_c = sch.add("gpsimd",
                                   (lambda r, s_, hf_: lambda g: g.dma_start(
                                       r, s_[:, hf_ * 1024:(hf_ + 1) * 1024]))(raw, src, hf),
                                   after=[raw_free[par]], dma=True)
                    wt = a3(WT0 + i * 2048 + hf * 1024, 1024)
                    ev_t = sch.add("scalar",
                                   (lambda w, r: lambda s: s.dma_start(
                                       w.rearrange("p (c f) -> p c f", c=8), r, transpose=True))(
                                       wt, raw),
                                   after=[ev_c, kqv_free[m]] + list(after_extra), dma=True)
                    raw_free[par] = ev_t
                    wt_tr[(m, h)] = ev_t

        emit_w_load(0, after_extra=[x_ready])

        for h in range(H):
            wtk = a3(WT0 + 0 * 2048, 2048)
            wtq = a3(WT0 + 1 * 2048, 2048)
            wtv = a3(WT0 + 2 * 2048, 2048)

            def kqv_mm_h(wt, hf):
                def fn(t):
                    inst = None
                    for dc in range(16):
                        for q4 in (2 * hf, 2 * hf + 1):
                            inst = t.matmul(ps_sc[:, q4 * 512:(q4 + 1) * 512],
                                            wt[:, dc * 128:(dc + 1) * 128],
                                            A2[:, dc * 2048 + q4 * 512: dc * 2048 + (q4 + 1) * 512],
                                            start=(dc == 0), stop=(dc == 15))
                    return inst
                return fn

            def proj(wt, m, evac_fn, prev_evs, extra=()):
                mms = []
                evs = []
                for hf in (0, 1):
                    ev_mm = sch.add("tensor", kqv_mm_h(wt, hf),
                                    after=[wt_tr[(m, h)], x_ready, prev_p1_half[hf],
                                           prev_evs[hf]] + list(extra), sig=True)
                    mms.append(ev_mm)
                    evs.append(evac_fn(hf, ev_mm))
                return mms, evs

            if prev_kev is None:
                prev_kev = [None, None]
                prev_qev = [None, None]
                prev_vev = [None, None]
            kmms, kevs = proj(wtk, "k",
                              lambda hf, mm: sch.add(
                                  "vector",
                                  (lambda hf_: lambda v: v.tensor_copy(
                                      K_sb[:, hf_ * 1024:(hf_ + 1) * 1024],
                                      ps_sc[:, hf_ * 1024:(hf_ + 1) * 1024]))(hf),
                                  after=[mm], sig=True),
                              prev_kev)
            prev_kev = kevs
            kqv_free["k"] = kmms[1]
            qmms, qevs = proj(wtq, "q",
                              lambda hf, mm: sch.add(
                                  "scalar",
                                  (lambda hf_: lambda s: s.activation(
                                      Q_sb[:, hf_ * 1024:(hf_ + 1) * 1024],
                                      ps_sc[:, hf_ * 1024:(hf_ + 1) * 1024],
                                      AF.Identity, scale=QSCALE))(hf),
                                  after=[mm], sig=True),
                              [kevs[0], kevs[1]])
            prev_qev = qevs
            kqv_free["q"] = qmms[1]
            vmms, vevs = proj(wtv, "v",
                              lambda hf, mm: sch.add(
                                  "vector",
                                  (lambda hf_: lambda v: v.tensor_copy(
                                      V_fm[:, hf_ * 1024:(hf_ + 1) * 1024],
                                      ps_sc[:, hf_ * 1024:(hf_ + 1) * 1024]))(hf),
                                  after=[mm, prev_pt_parity[0]], sig=True),
                              [qevs[0], qevs[1]])
            prev_vev = vevs
            kqv_free["v"] = vmms[1]
            ev_vmm = vmms[1]
            kqv_evts[h % 2] = ev_vmm
            ev_vev = vevs[1]
            ev_vtr = sch.add("scalar",
                             lambda s: s.dma_start(V_tm.rearrange("p (c f) -> p c f", c=16),
                                                   V_fm, transpose=True),
                             after=[vevs[0], vevs[1], prev_vtr, prev_zmm_parity[0],
                                    prev_zmm_parity[1]], dma=True)
            prev_vtr = ev_vtr

            if h + 1 < H:
                emit_w_load(h + 1)

            P_bufs = [P_bf, a3(PBF1, 2048)]
            PT_bufs = [PT_sb, PT_sb]

            scores_evs = {}
            p2_evs = {}

            def emit_scores(sq):
                kmax = 128 * (sq + 1)
                par = sq % 2
                jobs = [(0, min(1024, kmax))]
                if kmax > 1024:
                    jobs.append((1024, kmax))
                evs = []
                for (c0, c1) in jobs:
                    half = jcnt[0] % 2
                    jcnt[0] += 1
                    psJ = ps_sc[:, half * 1024: half * 1024 + 1024]

                    def fn(sq_, c0_, c1_, half_):
                        def run(t):
                            inst = None
                            psJ_ = ps_sc[:, half_ * 1024: half_ * 1024 + 1024]
                            for cc in range(c0_, c1_, 512):
                                cn = min(512, c1_ - cc)
                                inst = t.matmul(psJ_[:, cc - c0_: cc - c0_ + cn],
                                                Q_sb[:, sq_ * 128:(sq_ + 1) * 128],
                                                K_sb[:, cc:cc + cn], start=True, stop=True)
                            d0 = sq_ * 128
                            if c0_ <= d0 < c1_:
                                inst = t.matmul(psJ_[:, d0 - c0_: d0 - c0_ + 128],
                                                identb, mask_bf, start=False, stop=True)
                            return inst
                        return run
                    ev = sch.add("tensor", fn(sq, c0, c1, half),
                                 after=[prev_p1_half[half], vevs[0], vevs[1], qevs[0],
                                        qevs[1], ev_maskb], sig=True)
                    evs.append((c0, c1, half, ev))
                scores_evs[sq] = evs

            def emit_softmax(sq):
                kmax = 128 * (sq + 1)
                par = sq % 2
                base = ST_SM + 8 * par
                p1s = []
                for j, (c0, c1, half, ev_sc_j) in enumerate(scores_evs[sq]):
                    ev_p1 = sch.add("scalar",
                                    (lambda c0_, c1_, half_, j_, par_: lambda s: s.activation(
                                        P_bufs[par_][:, c0_:c1_],
                                        ps_sc[:, half_ * 1024: half_ * 1024 + (c1_ - c0_)],
                                        AF.Exp, bias=stc(ST_NEGSMAX),
                                        accum_out=stc(base + j_)))(c0, c1, half, j, par),
                                    after=[ev_sc_j, prev_pt_parity[par], ev_vtr, ev_cst],
                                    sig=True)
                    prev_p1_half[half] = ev_p1
                    p1s.append(ev_p1)
                if len(p1s) == 2:
                    ev_rs = sch.add("vector",
                                    (lambda b_: lambda v: v.tensor_add(stc(b_ + 2), stc(b_),
                                                                       stc(b_ + 1)))(base),
                                    after=p1s, sig=True)
                    lnsrc = base + 2
                else:
                    ev_rs = None
                    lnsrc = base
                ev_ln = sch.add("scalar",
                                (lambda b_, l_: lambda s: s.activation(stc(b_ + 3), stc(l_),
                                                                       AF.Ln))(base, lnsrc),
                                after=[ev_rs] + p1s, sig=True)
                ev_rcp = sch.add("scalar",
                                 (lambda b_: lambda s: s.activation(stc(b_ + 4), stc(b_ + 3),
                                                                    AF.Exp, scale=-1.0))(base),
                                 sig=True)
                ev_p2 = sch.add("vector",
                                (lambda kmax_, par_, b_: lambda v: v.tensor_scalar_mul(
                                    P_bufs[par_][:, 0:kmax_], P_bufs[par_][:, 0:kmax_],
                                    stc(b_ + 4)))(kmax, par, base),
                                after=[ev_rcp] + p1s, sig=True)
                p2_evs[sq] = ev_p2

            def emit_tail(sq, h_):
                kmax = 128 * (sq + 1)
                par = sq % 2
                if sq <= 7:
                    reg0 = par * 1024
                    ptdeps = [prev_pte_parity[par]]
                else:
                    reg0 = 0
                    ptdeps = [prev_pte_parity[0], prev_pte_parity[1]]
                ev_pt = sch.add("tensor",
                                (lambda sq_, par_, r0_: lambda t: [
                                    t.transpose(ps_pt[:, r0_ + mc * 128: r0_ + (mc + 1) * 128],
                                                P_bufs[par_][:, mc * 128:(mc + 1) * 128], identb)
                                    for mc in range(sq_ + 1)][-1])(sq, par, reg0),
                                after=[p2_evs[sq], ev_id] + ptdeps, sig=True)
                prev_pt_parity[par] = ev_pt
                if sq % 2 == 0:
                    ev_pte = sch.add("vector",
                                     (lambda kmax_, par_, r0_: lambda v: v.tensor_copy(
                                         PT_bufs[par_][:, 0:kmax_],
                                         ps_pt[:, r0_: r0_ + kmax_]))(kmax, par, reg0),
                                     after=[ev_pt, prev_zmm_parity[0], prev_zmm_parity[1]],
                                     sig=True)
                else:
                    ev_pte = sch.add("scalar",
                                     (lambda kmax_, par_, r0_: lambda s: s.copy(
                                         PT_bufs[par_][:, 0:kmax_],
                                         ps_pt[:, r0_: r0_ + kmax_]))(kmax, par, reg0),
                                     after=[ev_pt, prev_zmm_parity[0], prev_zmm_parity[1]],
                                     sig=True)
                prev_pte_parity[par] = ev_pte
                slot = sq % 2
                zreg = (ps_z[:, 0:128] if slot == 0 else ps_ms[:, 256:384])
                ev_zmm = sch.add("tensor",
                                 (lambda sq_, par_, zr: lambda t: [
                                     t.matmul(zr, V_tm[:, mc * 128:(mc + 1) * 128],
                                              PT_bufs[par_][:, mc * 128:(mc + 1) * 128],
                                              start=(mc == 0), stop=(mc == sq_))
                                     for mc in range(sq_ + 1)][-1])(sq, par, zreg),
                                 after=[ev_pte] + prev_zev_slot[slot], sig=True)
                prev_zmm_parity[par] = ev_zmm
                ev_zev = sch.add("vector",
                                 (lambda h2, sq_, zr: lambda v: v.tensor_copy(
                                     A1[:, sq_ * 2048 + h2 * 128: sq_ * 2048 + (h2 + 1) * 128],
                                     zr))(h_, sq, zreg),
                                 after=[ev_zmm], sig=True)
                prev_zev_slot[slot] = [ev_zev]
                prev_zev_any[0] = ev_zev
                z_done[sq].append(ev_zev)

            for i in range(NSC + 2):
                if 0 <= i - 1 < NSC:
                    emit_softmax(i - 1)
                if 0 <= i - 2 < NSC:
                    emit_tail(i - 2, h)
                if i < NSC:
                    emit_scores(i)
                drive_prepass(2)


        def drive_until_group(g):
            while prep_left[0] and g not in prep["grp"]:
                drive_prepass(1)
        while prep_left[0] and prep.get("s23") is None:
            drive_prepass(1)
        ev_t1 = prep["t"][1]
        ev_s23 = prep["s23"]

        prev_zev = prev_zev_any[0]
        prev_p2 = p2_evs[NSC - 1]
        prev_ptev = prev_pte_parity[1]
        ph1_done = [prev_zev, prev_p2, prev_pte_parity[0], prev_pte_parity[1],
                    prev_p1_half[0], prev_p1_half[1], prev_pt_parity[0], prev_pt_parity[1],
                    kqv_evts[0], kqv_evts[1]]

        if debug:
            dbg_z = nc.dram_tensor("dbg_z", [128, 32768], BF16, kind="ExternalOutput").ap()
            dbg_k = nc.dram_tensor("dbg_k", [128, 2048], BF16, kind="ExternalOutput").ap()
            dbg_q = nc.dram_tensor("dbg_q", [128, 2048], BF16, kind="ExternalOutput").ap()
            dbg_v = nc.dram_tensor("dbg_v", [128, 2048], BF16, kind="ExternalOutput").ap()
            dbg_p = nc.dram_tensor("dbg_p", [128, 2048], BF16, kind="ExternalOutput").ap()
            dbg_pt = nc.dram_tensor("dbg_pt", [128, 2048], BF16, kind="ExternalOutput").ap()
            dbg_st = nc.dram_tensor("dbg_st", [128, 640], F32, kind="ExternalOutput").ap()
            evs = []
            for nm, dst, srcb in (("z", dbg_z, A1[:, :]), ("k", dbg_k, K_sb), ("q", dbg_q, Q_sb),
                                  ("v", dbg_v, V_tm), ("p", dbg_p, P_bf), ("pt", dbg_pt, PT_sb),
                                  ("st", dbg_st, smalls[:, :])):
                evs.append(sch.add("gpsimd",
                                   (lambda d, s_: lambda g: g.dma_start(d, s_))(dst, srcb),
                                   after=ph1_done, dma=True))
            ph1_done = ph1_done + evs

        woT_tr = []
        for h in range(H):
            src = wo_in[h]
            for hf in range(2):
                par = (2 * h + hf) % 2
                raw = a3(RAW0 + par * 1024, 1024)
                src_ap = bass.AP(src.tensor, src.offset + hf * 8 * 16384,
                                 [[128, 128], [16384, 8], [1, 128]])
                ev_c = sch.add("gpsimd",
                               (lambda r, sap: lambda g: g.dma_start(
                                   r.rearrange("p (c f) -> p c f", c=8), sap))(raw, src_ap),
                               after=[raw_free[par]], dma=True)
                ev_t = sch.add("scalar",
                               (lambda h_, r, hf_: lambda s: s.dma_start(
                                   a2(h_ * 2048 + hf_ * 1024, 1024).rearrange(
                                       "p (c f) -> p c f", c=8), r,
                                   transpose=True))(h, raw, hf),
                               after=[ev_c, kqv_evts[0], kqv_evts[1]], dma=True)
                raw_free[par] = ev_t
                woT_tr.append(ev_t)

        X320, R0, SQD0, YTM0 = 0, 8192, 16384, 20480
        prev_x32 = [None, None]
        prev_r_half = [None, None]
        y_free = [[None, None], [None, None]]
        prev_r = None
        y_arena_ev = {}
        y32_ev = {}
        ln1_sq_last = None
        ln1_y_last = None
        last_amm = None
        for sc in range(NSC):
            par = sc % 2
            x32 = a3f(X320 + par * 4096, 2048)
            r_sb = a3f(R0 + par * 4096, 2048)
            sqd = x32
            y_tm = a3f(YTM0 + par * 4096, 2048)
            ls = ST_LN + 8 * par
            ev_x = sch.add("sync",
                           (lambda b, s0: lambda y: y.dma_start(b, x_in[s0:s0 + 128, :]))(
                               x32, sc * 128),
                           after=[prev_x32[par]] + (ph1_done if sc < 2 else []), dma=True)

            def attn_mm(sc_, hf_):
                def fn(t):
                    inst = None
                    for h in range(H):
                        for q4 in (2 * hf_, 2 * hf_ + 1):
                            inst = t.matmul(ps_sc[:, q4 * 512:(q4 + 1) * 512],
                                            A1[:, sc_ * 2048 + h * 128: sc_ * 2048 + (h + 1) * 128],
                                            a2(h * 2048 + q4 * 512, 512),
                                            start=(h == 0), stop=(h == 15))
                    return inst
                return fn
            r_evs = []
            for hf in (0, 1):
                ev_amm = sch.add("tensor", attn_mm(sc, hf),
                                 after=[prev_r_half[hf], prev_p1_half[0], prev_p1_half[1]] +
                                       z_done[sc] + woT_tr, sig=True)
                ev_rh = sch.add("vector",
                                (lambda r_, x_, hf2: lambda v: v.tensor_add(
                                    r_.bitcast(F32) if False else r_,
                                    ps_sc[:, hf2 * 1024:(hf2 + 1) * 1024], x_))(
                                    a3f(R0 + par * 4096 + hf * 2048, 1024),
                                    a3f(X320 + par * 4096 + hf * 2048, 1024), hf),
                                after=[ev_amm, ev_x], sig=True)
                prev_r_half[hf] = ev_rh
                r_evs.append(ev_rh)
            last_amm = ev_amm
            ev_r = r_evs[1]
            prev_r = ev_r
            ev_l1 = sch.add("vector",
                            (lambda r_, l_: lambda v: v.reduce_sum(stc(l_), r_, axis=AX.X))(r_sb, ls),
                            after=r_evs, sig=True)
            ev_l2 = sch.add("vector",
                            (lambda l_: lambda v: v.tensor_scalar_mul(stc(l_ + 1), stc(l_),
                                                                      -1.0 / D))(ls),
                            after=[ev_l1], sig=True, fence=True)
            ev_l3 = sch.add("scalar",
                            (lambda r_, q_, l_: lambda s: s.activation(q_, r_, AF.Square,
                                                                       bias=stc(l_ + 1),
                                                                       accum_out=stc(l_ + 2)))(
                                r_sb, sqd, ls),
                            after=[ev_l2, ln1_sq_last], sig=True)
            ev_l4 = ev_l3
            ln1_sq_last = ev_l3
            ev_l5 = sch.add("vector",
                            (lambda l_: lambda v: v.tensor_scalar(stc(l_ + 3), stc(l_ + 2),
                                                                  1.0 / D, EPS, ALU.mult, ALU.add))(ls),
                            after=[ev_l4], sig=True, fence=True)
            ev_l6 = sch.add("scalar",
                            (lambda l_: lambda s: s.activation(stc(l_ + 4), stc(l_ + 3), AF.Ln))(ls),
                            after=[ev_l5], sig=True)
            ev_l7 = sch.add("scalar",
                            (lambda l_: lambda s: s.activation(stc(l_ + 5), stc(l_ + 4), AF.Exp,
                                                               scale=-0.5))(ls),
                            after=[ev_l6], sig=True, fence=True)
            ev_l8 = sch.add("vector",
                            (lambda l_: lambda v: v.tensor_mul(stc(l_ + 6), stc(l_ + 1),
                                                               stc(l_ + 5)))(ls),
                            after=[ev_l7], sig=True)
            ev_y = sch.add("scalar",
                           (lambda r_, y_, l_: lambda s: s.activation(y_, r_, AF.Identity,
                                                                      scale=stc(l_ + 5),
                                                                      bias=stc(l_ + 6)))(
                               r_sb, y_tm, ls),
                           after=[ev_l8] + y_free[par], sig=True)
            ln1_y_last = ev_y
            ev_yh = sch.add("vector",
                            (lambda y_, sc_: lambda v: v.tensor_copy(
                                A1[:, sc_ * 2048:(sc_ + 1) * 2048], y_))(y_tm, sc),
                            after=[ev_y], sig=True)
            y_arena_ev[sc] = ev_yh
            ev_ysp = sch.add("gpsimd",
                             (lambda y_, s0: lambda g: g.dma_start(y32_s[s0:s0 + 128, :], y_))(
                                 y_tm, sc * 128),
                             after=[ev_y], dma=True)
            y32_ev[sc] = ev_ysp
            y_free[par] = [ev_yh, ev_ysp]
            prev_x32[par] = ev_l3
            drive_prepass(45)

        ph2_done = [last_amm, prev_r, ln1_y_last, y_free[0][0], y_free[0][1],
                    y_free[1][0], y_free[1][1]]

        YFM0, FTM0, W2R0, SIL0, GR0, OUT0 = 0, 8192, 16384, 24576, 26624, 28672
        FACC0, Y32R0, W1B0, W3B0 = 0, 8192, 20480, 24576

        f_acc = A3[:, FACC0:FACC0 + 8192].bitcast(F16)
        prev_y32r = [None, None]
        prev_w1b = [None, None]
        prev_w3b = [None, None]
        prev_w2r = [None, None, None, None]
        prev_gr = [None, None, None, None]
        prev_sil = [None, None]
        prev_h1 = None
        prev_h3 = None
        prev_fa_slot = [None, None, None, None]
        prev_facc_free = None
        ln2_sq_last = None
        prev_out_ev = None
        prev_yfm = None
        prev_ftm = None
        first3 = True

        for b in range(4):
            sc0 = b * 4
            yfm_evs = []
            for k in range(4):
                sc = sc0 + k
                in_ap = A1[:, sc * 2048:(sc + 1) * 2048]
                out_ap = bass.AP(A2, YFM0 + k * 128, [[32768, 128], [512, 16], [1, 128]])
                ev = sch.add("scalar",
                             (lambda o, i: lambda s: s.dma_start(o, i, transpose=True))(out_ap, in_ap),
                             after=[y_arena_ev[sc], prev_yfm] + (ph2_done if first3 else []),
                             dma=True)
                yfm_evs.append(ev)
            prev_yfm = yfm_evs[-1]
            yfm_last = yfm_evs[-1]
            ev_fz = sch.add("vector", lambda v: v.memset(f_acc[:, :], 0.0),
                            after=[prev_facc_free] + (ph2_done if first3 else []), sig=True)

            if b == 0:
                while prep_left[0]:
                    drive_prepass(50)
                prep_all = [prep["grp"][g][-1] for g in prep["grp"]] + [pbe[0], pbe[1]]
            groups = [list(range(i, min(i + 4, NFC))) for i in range(0, NFC, 4)]
            for gi_, grp in enumerate(groups):
                grp_deps = prep_all if b == 0 else []
                g_ready = {}
                for fc in grp:
                    slot = fc % 4
                    parw = fc % 2
                    w1b = a3(W1B0 + parw * 2048, 2048)
                    w3b = a3(W3B0 + parw * 2048, 2048)
                    w2b = a2(W2R0 + slot * 2048, 2048)
                    silu_t = a2f(SIL0 + parw * 1024, 512)
                    gbuf = a2(GR0 + slot * 512, 512)
                    ev_w1 = sch.add("sync",
                                    (lambda w, fc_: lambda y: y.dma_start(w, w1qT_s[fc_]))(w1b, fc),
                                    after=grp_deps + [prev_w1b[parw]] +
                                          (ph2_done if first3 else []), dma=True)
                    ev_w3 = sch.add("sync",
                                    (lambda w, fc_: lambda y: y.dma_start(w, w3qT_s[fc_]))(w3b, fc),
                                    after=grp_deps + [prev_w3b[parw]], dma=True)
                    ev_w2 = sch.add("gpsimd",
                                    (lambda w, fc_: lambda g: g.dma_start(w, w2qT_s[fc_]))(w2b, fc),
                                    after=grp_deps + [prev_w2r[slot]] +
                                          (ph2_done if first3 else []), dma=True)
                    first3 = False

                    def h_mm(wb, ps_ap):
                        def fn(t):
                            inst = None
                            for dc in range(16):
                                inst = t.matmul(ps_ap, wb[:, dc * 128:(dc + 1) * 128],
                                                a2(YFM0 + dc * 512, 512),
                                                start=(dc == 0), stop=(dc == 15))
                            return inst
                        return fn
                    ev_h1 = sch.add("tensor", h_mm(w1b, ps_z[:, 0:512]),
                                    after=[ev_w1, yfm_last, prev_h1,
                                           (prev_zev if b == 0 else None)], sig=True)
                    prev_w1b[parw] = ev_h1
                    ev_sl = sch.add("scalar",
                                    (lambda st_: lambda s: s.activation(st_, ps_z[:, 0:512],
                                                                        AF.Silu,
                                                                        scale=stc(ST_THR[1])))(silu_t),
                                    after=[ev_h1, ev_t1, prev_sil[parw]], sig=True)
                    prev_h1 = ev_sl
                    ev_h3 = sch.add("tensor", h_mm(w3b, ps_h3),
                                    after=[ev_w3, yfm_last, prev_h3,
                                           (prev_ptev if b == 0 else None),
                                           (prev_zev_any[0] if b == 0 else None),
                                           (prep["t"][2] if b == 0 else None)], sig=True)
                    prev_w3b[parw] = ev_h3
                    ev_g = sch.add("vector",
                                   (lambda gb, st_: lambda v: v.tensor_tensor(gb, st_, ps_h3,
                                                                              ALU.mult))(
                                       gbuf, silu_t),
                                   after=[ev_h3, ev_sl, prev_gr[slot]], sig=True)
                    prev_h3 = ev_g
                    prev_sil[parw] = ev_g
                    g_ready[fc] = (ev_g, ev_w2)

                for dc in range(16):
                    slot4 = dc % 4

                    def f_mm(grp_, dc_):
                        def fn(t):
                            inst = None
                            for gi, fc in enumerate(grp_):
                                sl = fc % 4
                                inst = t.matmul(ps_sc[:, (dc_ % 4) * 512:(dc_ % 4) * 512 + 512],
                                                a2(W2R0 + sl * 2048, 2048)[:, dc_ * 128:(dc_ + 1) * 128],
                                                a2(GR0 + sl * 512, 512),
                                                start=(gi == 0), stop=(gi == len(grp_) - 1))
                            return inst
                        return fn
                    deps = [g_ready[fc][0] for fc in grp] + [g_ready[fc][1] for fc in grp]
                    ev_f = sch.add("tensor", f_mm(grp, dc),
                                   after=deps + [prev_fa_slot[slot4],
                                                 (prev_r if b == 0 and dc < 4 else None)], sig=True)
                    ev_fa = sch.add("vector",
                                    (lambda dc_: lambda v: v.tensor_add(
                                        f_acc[:, dc_ * 512:(dc_ + 1) * 512],
                                        f_acc[:, dc_ * 512:(dc_ + 1) * 512],
                                        ps_sc[:, (dc_ % 4) * 512:(dc_ % 4) * 512 + 512]))(dc),
                                    after=[ev_f, ev_fz], sig=True)
                    prev_fa_slot[slot4] = ev_fa
                last_fa = prev_fa_slot[15 % 4]
                for fc in grp:
                    prev_w2r[fc % 4] = last_fa
                    prev_gr[fc % 4] = last_fa
            facc_final = [e for e in prev_fa_slot if e is not None]

            ftm_evs = []
            for dc in range(16):
                in_ap = A3[:, FACC0 + dc * 512:FACC0 + (dc + 1) * 512].bitcast(F16)
                out_ap = bass.AP(A2, FTM0 + dc * 128, [[32768, 128], [2048, 4], [1, 128]]).bitcast(F16)
                ev = sch.add("scalar",
                             (lambda o, i: lambda s: s.dma_start(o, i, transpose=True))(out_ap, in_ap),
                             after=facc_final + [prev_ftm], dma=True)
                ftm_evs.append(ev)
            prev_ftm = ftm_evs[-1]
            prev_facc_free = ftm_evs[-1]

            for k in range(4):
                sc = sc0 + k
                par = sc % 2
                y32b = a3f(Y32R0 + par * 4096, 2048)
                outst = a2f(OUT0, 2048)
                sqd4 = outst
                ls = ST_LN + 8 * par
                ev_yl = sch.add("sync",
                                (lambda b_, s0: lambda y: y.dma_start(b_, y32_s[s0:s0 + 128, :]))(
                                    y32b, sc * 128),
                                after=[y32_ev[sc], prev_y32r[par]] +
                                      (ph2_done if b == 0 and k < 2 else []), dma=True)
                ftm_sc = A2[:, FTM0 + k * 2048:FTM0 + (k + 1) * 2048].bitcast(F16)
                ev_r2 = sch.add("vector",
                                (lambda y_, f_: lambda v: v.scalar_tensor_tensor(
                                    y_, f_, stc(ST_S23), y_, ALU.mult, ALU.add))(y32b, ftm_sc),
                                after=[ev_yl, ftm_evs[-1], ev_s23], sig=True)
                prev_y32r[par] = ev_r2
                ev_m1 = sch.add("vector",
                                (lambda y_, l_: lambda v: v.reduce_sum(stc(l_), y_, axis=AX.X))(
                                    y32b, ls),
                                after=[ev_r2], sig=True, fence=True)
                ev_m2 = sch.add("vector",
                                (lambda l_: lambda v: v.tensor_scalar_mul(stc(l_ + 1), stc(l_),
                                                                          -1.0 / D))(ls),
                                after=[ev_m1], sig=True, fence=True)
                ev_m3 = sch.add("scalar",
                                (lambda r_, q_, l_: lambda s: s.activation(q_, r_, AF.Square,
                                                                           bias=stc(l_ + 1),
                                                                           accum_out=stc(l_ + 2)))(
                                    y32b, sqd4, ls),
                                after=[ev_m2, ln2_sq_last, prev_out_ev], sig=True)
                ev_m4 = ev_m3
                ln2_sq_last = ev_m3
                ev_m5 = sch.add("vector",
                                (lambda l_: lambda v: v.tensor_scalar(stc(l_ + 3), stc(l_ + 2),
                                                                      1.0 / D, EPS, ALU.mult,
                                                                      ALU.add))(ls),
                                after=[ev_m4], sig=True, fence=True)
                ev_m6 = sch.add("scalar",
                                (lambda l_: lambda s: s.activation(stc(l_ + 4), stc(l_ + 3),
                                                                   AF.Ln))(ls),
                                after=[ev_m5], sig=True)
                ev_m7 = sch.add("scalar",
                                (lambda l_: lambda s: s.activation(stc(l_ + 5), stc(l_ + 4),
                                                                   AF.Exp, scale=-0.5))(ls),
                                after=[ev_m6], sig=True, fence=True)
                ev_m8 = sch.add("vector",
                                (lambda l_: lambda v: v.tensor_mul(stc(l_ + 6), stc(l_ + 1),
                                                                   stc(l_ + 5)))(ls),
                                after=[ev_m7], sig=True)
                ev_o = sch.add("scalar",
                               (lambda r_, o_, l_: lambda s: s.activation(o_, r_, AF.Identity,
                                                                          scale=stc(l_ + 5),
                                                                          bias=stc(l_ + 6)))(
                                   y32b, outst, ls),
                               after=[ev_m8, prev_out_ev], sig=True)
                ev_st = sch.add("sync",
                                (lambda o_, s0: lambda y: y.dma_start(out_hbm[s0:s0 + 128, :], o_))(
                                    outst, sc * 128),
                                after=[ev_o], dma=True)
                prev_out_ev = ev_st

        if debug:
            dbg_a2 = nc.dram_tensor("dbg_a2", [128, 32768], BF16, kind="ExternalOutput").ap()
            dbg_a3 = nc.dram_tensor("dbg_a3", [128, 28672], BF16, kind="ExternalOutput").ap()
            sch.add("gpsimd", lambda g: g.dma_start(dbg_a2, A2[:, :]),
                    after=[prev_out_ev, prev_ftm, prev_facc_free], dma=True)
            sch.add("gpsimd", lambda g: g.dma_start(dbg_a3, A3[:, :]),
                    after=[prev_out_ev], dma=True)
        final_val = sch.counts["d_y"]
        final_valg = sch.counts["d_g"]
        sch.ops.append(("sync", lambda y: None, [("d_y", final_val), ("d_g", final_valg)], None, False))

        sch.check()

        with nc.Block() as block:
            def emit_engine(decor, ename):
                ops = [(f, w, i, d) for (ee, f, w, i, d) in sch.ops if ee == ename]

                @decor
                def _(e):
                    for fn, waits, inc, dma in ops:
                        for s_, v_ in waits:
                            e.wait_ge(sems[s_], v_)
                        inst = fn(e)
                        if inc is not None:
                            s_, v_ = inc
                            if dma:
                                inst.then_inc(sems[s_], 16)
                            else:
                                if inst is None:
                                    inst = e.nop()
                                inst.then_inc(sems[s_], 1)

            emit_engine(block.tensor, "tensor")
            emit_engine(block.vector, "vector")
            emit_engine(block.scalar, "scalar")
            emit_engine(block.gpsimd, "gpsimd")
            emit_engine(block.sync, "sync")

    return nc


_NC_CACHE = None


def kernel(L0_attn_x, L0_attn_W_Q, L0_attn_W_K, L0_attn_W_V, L0_attn_W_O,
           L0_ffn_w1, L0_ffn_w3, L0_ffn_w2):
    global _NC_CACHE
    x = np.ascontiguousarray(np.asarray(L0_attn_x, dtype=np.float32))
    ws = {
        "wq": np.ascontiguousarray(np.asarray(L0_attn_W_Q, dtype=np.float32)),
        "wk": np.ascontiguousarray(np.asarray(L0_attn_W_K, dtype=np.float32)),
        "wv": np.ascontiguousarray(np.asarray(L0_attn_W_V, dtype=np.float32)),
        "wo": np.ascontiguousarray(np.asarray(L0_attn_W_O, dtype=np.float32)),
        "w1": np.ascontiguousarray(np.asarray(L0_ffn_w1, dtype=np.float32)),
        "w3": np.ascontiguousarray(np.asarray(L0_ffn_w3, dtype=np.float32)),
        "w2": np.ascontiguousarray(np.asarray(L0_ffn_w2, dtype=np.float32)),
    }
    if _NC_CACHE is None:
        _NC_CACHE = build_nc()
    in_maps = [dict(x=np.ascontiguousarray(x[b]), **ws) for b in range(8)]
    res = run_bass_kernel_spmd(_NC_CACHE, in_maps, core_ids=list(range(8)))
    return np.stack([res.results[b]["out"] for b in range(8)], axis=0)
